# version 13
# speedup vs baseline: 1.2607x; 1.0754x over previous
import contextlib
import math
import numpy as np
import ml_dtypes
import concourse.bass as bass
import concourse.mybir as mybir

F32 = mybir.dt.float32
BF16 = mybir.dt.bfloat16
I32 = mybir.dt.int32
AF = mybir.ActivationFunctionType
ALU = mybir.AluOpType
AX = mybir.AxisListType

ENGS = ("sync", "vector", "scalar", "tensor", "gpsimd")


class Ctr:
    __slots__ = ("h", "v", "name")

    def __init__(self, h, name):
        self.h = h
        self.v = 0
        self.name = name


class KB:
    def __init__(self, name="k"):
        self.nc = bass.Bass("TRN2", target_bir_lowering=False, debug=False)
        self.ops = []
        self.stack = contextlib.ExitStack()
        self.stack.__enter__()
        self._sems = {}
        self._sbuf_bytes = 0
        self.phase = "init"
        self.pg = {e: self.sem("pg_" + e) for e in ENGS}

    def sem(self, name):
        c = Ctr(self.stack.enter_context(self.nc.semaphore(name)), name)
        self._sems[name] = c
        return c

    def sb(self, name, shape, dt):
        npart, free = shape
        isz = 2 if dt in (BF16, mybir.dt.float16, mybir.dt.int16) else 4
        self._sbuf_bytes += free * isz
        st = getattr(self, "_scope", None) or self.stack
        t = st.enter_context(self.nc.sbuf_tensor(name, list(shape), dt))
        return t

    def ps(self, name, shape, dt=F32):
        st = getattr(self, "_scope", None) or self.stack
        return st.enter_context(self.nc.psum_tensor(name, list(shape), dt))

    @contextlib.contextmanager
    def scope(self):
        prev = getattr(self, "_scope", None)
        saved = self._sbuf_bytes
        with contextlib.ExitStack() as es:
            self._scope = es
            try:
                yield
            finally:
                self._scope = prev
        self._sbuf_bytes = saved
        self.barrier()

    def dram(self, name, shape, dt, kind="Internal"):
        return self.nc.dram_tensor(name, list(shape), dt, kind=kind).ap()

    def const(self, name, arr):
        return self.nc.inline_tensor(np.ascontiguousarray(arr), name=name).ap()

    def do(self, eng, fn, waits=(), inc=None, n=1, tag=""):
        ws = []
        for w in waits:
            if isinstance(w, Ctr):
                ws.append((w.h, w.v))
            else:
                c, v = w
                ws.append((c.h, v))
        incs = []
        if inc is not None:
            items = inc if isinstance(inc, list) else [(inc, n)]
            for c, k in items:
                c.v += k
                incs.append((c.h, k))
        self.ops.append((eng, ws, fn, incs, tag, self.phase))

    def dmago(self, q, fn, sem, waits=(), tag=""):
        self.do(q, fn, waits=waits, inc=sem, n=16, tag=tag)
        return (sem, sem.v)

    def barrier(self, engs=ENGS):
        excl = getattr(self, "barrier_exclude", set())
        snap = [(c, c.v) for c in self._sems.values() if c.v > 0 and c.name not in excl]
        for e in engs:
            for (c, v) in snap:
                self.ops.append((e, [(c.h, v)], None, [], "barrier", self.phase))

    def build(self):
        nc = self.nc
        assert self._sbuf_bytes <= 224 * 1024, f"SBUF over budget: {self._sbuf_bytes}"
        ops = self.ops
        with nc.Block() as block:
            for name in ENGS:
                def mk(name):
                    def body(e):
                        import contextlib as _cl
                        cur_ph = [None]
                        stack = [None]
                        def set_ph(ph):
                            if ph == cur_ph[0]:
                                return
                            if stack[0] is not None:
                                stack[0].__exit__(None, None, None)
                                stack[0] = None
                            cur_ph[0] = ph
                            if ph is not None:
                                stack[0] = nc.named_scope(ph)
                                stack[0].__enter__()
                        for (en, ws, fn, incs, tag, ph) in ops:
                            if en != name:
                                continue
                            set_ph(ph)
                            for (h, v) in ws:
                                e.wait_ge(h, v)
                            if fn is None:
                                continue
                            inst = fn(e)
                            for (h, k) in incs:
                                inst.then_inc(h, k)
                        set_ph(None)
                    return body
                getattr(block, name)(mk(name))
        self.stack.__exit__(None, None, None)
        return nc



def ceil_div(a, b):
    return (a + b - 1) // b


def ph_prelude(k: KB, T, x_ap, wr_ap, xnfm32_ap, xnfm16_ap, logits_ap, xntok16_ap=None):
    nc, do, dma = k.nc, k.do, k.dmago
    nt = T // 128
    D = 1024
    nchunk = D // 128

    xt = [k.sb(f"pa_xt{p}", [128, D], F32) for p in range(2)]
    sq = [k.sb(f"pa_sq{p}", [128, D], F32) for p in range(2)]
    st = [k.sb(f"pa_st{p}", [128, 8], F32) for p in range(2)]
    xnt = [k.sb(f"pa_xnt{p}", [128, D], F32) for p in range(2)]
    fm32 = [k.sb(f"pa_fm32{p}", [128, D], F32) for p in range(2)]
    fm16 = [k.sb(f"pa_fm16{p}", [128, D], BF16) for p in range(2)]
    tok16 = [k.sb(f"pa_tok16{p}", [128, D], BF16) for p in range(2)]
    lst = [k.sb(f"pa_lst{p}", [128, 8], F32) for p in range(2)]
    wr_sb = k.sb("pa_wr", [8, D], F32)
    wr16 = k.sb("pa_wr16", [8, D], BF16)
    wrT = k.sb("pa_wrT", [128, 8 * nchunk], BF16)
    idf = k.sb("pa_idf", [128, 128], F32)
    idb = k.sb("pa_idb", [128, 128], BF16)
    psT = [k.ps(f"pa_psT{p}", [128, 128], F32) for p in range(2)]
    psL = [k.ps(f"pa_psL{p}", [128, 8], F32) for p in range(2)]
    psW = k.ps("pa_psW", [128, 8], BF16)

    pv, ps_, pt, pg = k.pg["vector"], k.pg["scalar"], k.pg["tensor"], k.pg["gpsimd"]
    s_wr = k.sem("pa_s_wr")
    s_ldx = [k.sem(f"pa_s_ldx{p}") for p in range(2)]
    s_st = [k.sem(f"pa_s_st{p}") for p in range(2)]

    if xntok16_ap is not None:
        ztail = k.sb("pa_ztail", [128, D], BF16)
        s_zt = k.sem("pa_szt")
        do("gpsimd", lambda e: e.memset(ztail[:], 0.0), inc=pg)
        dma("gpsimd", lambda e: e.dma_start(out=xntok16_ap[T:T + 128, :], in_=ztail[:, :]),
            s_zt, waits=[(pg, pg.v)])
    ld0 = dma("sync", lambda e: e.dma_start(out=wr_sb[:, :], in_=wr_ap[:, :]), s_wr, tag="wr_ld")
    do("gpsimd", lambda e: e.memset(idf[:], 0.0), inc=pg)
    do("gpsimd", lambda e: e.affine_select(out=idf[:], in_=idf[:], compare_op=ALU.not_equal,
                                           fill=1.0, base=0, pattern=[[-1, 128]], channel_multiplier=1),
       waits=[pg], inc=pg, tag="ident")
    do("gpsimd", lambda e: e.memset(idb[:], 0.0), waits=[pg], inc=pg)
    do("gpsimd", lambda e: e.affine_select(out=idb[:], in_=idb[:], compare_op=ALU.not_equal,
                                           fill=1.0, base=0, pattern=[[-1, 128]], channel_multiplier=1),
       waits=[pg], inc=pg, tag="identb")
    ident_done = (pg, pg.v)
    do("vector", lambda e: e.tensor_copy(out=wr16[:], in_=wr_sb[:]),
       waits=[ld0], inc=pv, tag="wr_cast")
    wr_cast = (pv, pv.v)
    for c in range(nchunk):
        do("tensor", lambda e, c=c: e.transpose(out=psW[:, :], in_=wr16[:, c * 128:(c + 1) * 128],
                                                identity=idb[:8, :8]),
           waits=[wr_cast, ident_done] + ([(ps_, ps_.v)] if c else []), inc=pt, tag="wrT")
        do("scalar", lambda e, c=c: e.copy(out=wrT[:, c * 8:(c + 1) * 8], in_=psW[:, :]),
           waits=[(pt, pt.v)], inc=ps_, tag="wrT_cp")
    wrT_done = (ps_, ps_.v)

    xn_done = {}
    evac_done = {}
    store_done = {}
    sqd = {}
    mml_done = {}
    tp_done = {}
    for i in range(nt):
        p = i % 2
        w = [xn_done[i - 2], sqd[i - 2]] if i >= 2 else []
        ldw = dma("sync", lambda e, i=i, p=p: e.dma_start(out=xt[p][:, :], in_=x_ap[i * 128:(i + 1) * 128, :]),
                  s_ldx[p], waits=w, tag=f"ldx{i}")
        do("vector", lambda e, p=p: e.reduce_sum(out=st[p][:, 0:1], in_=xt[p][:, :], axis=AX.X),
           waits=[ldw], inc=pv)
        do("scalar", lambda e, p=p: e.activation(out=sq[p][:, :], in_=xt[p][:, :], func=AF.Square,
                                                 accum_out=st[p][:, 1:2]),
           waits=[ldw] + ([(ps_, ps_.v)] if i >= 2 else []), inc=ps_)
        sq_done = (ps_, ps_.v)
        sqd[i] = sq_done
        do("vector", lambda e, p=p: e.tensor_scalar(out=st[p][:, 2:3], in0=st[p][:, 0:1],
                                                    scalar1=1.0 / D, scalar2=None, op0=ALU.mult),
           waits=[(pv, pv.v)], inc=pv)
        do("vector", lambda e, p=p: e.tensor_scalar(out=st[p][:, 3:4], in0=st[p][:, 1:2],
                                                    scalar1=1.0 / D, scalar2=None, op0=ALU.mult),
           waits=[sq_done, (pv, pv.v)], inc=pv)
        do("vector", lambda e, p=p: e.scalar_tensor_tensor(out=st[p][:, 4:5], in0=st[p][:, 2:3],
                                                           scalar=1.0, in1=st[p][:, 2:3],
                                                           op0=ALU.mult, op1=ALU.mult),
           waits=[(pv, pv.v)], inc=pv)
        do("vector", lambda e, p=p: e.tensor_sub(out=st[p][:, 3:4], in0=st[p][:, 3:4], in1=st[p][:, 4:5]),
           waits=[(pv, pv.v)], inc=pv)
        do("vector", lambda e, p=p: e.tensor_scalar(out=st[p][:, 3:4], in0=st[p][:, 3:4],
                                                    scalar1=1e-5, scalar2=None, op0=ALU.add),
           waits=[(pv, pv.v)], inc=pv)
        var_done = (pv, pv.v)
        do("scalar", lambda e, p=p: e.activation(out=st[p][:, 5:6], in_=st[p][:, 3:4], func=AF.Sqrt),
           waits=[var_done], inc=ps_)
        do("vector", lambda e, p=p: e.reciprocal(out=st[p][:, 6:7], in_=st[p][:, 5:6]),
           waits=[(ps_, ps_.v)], inc=pv)
        do("vector", lambda e, p=p: e.tensor_scalar(out=xnt[p][:, :], in0=xt[p][:, :],
                                                    scalar1=st[p][:, 2:3], scalar2=st[p][:, 6:7],
                                                    op0=ALU.subtract, op1=ALU.mult),
           waits=[(pv, pv.v)] + ([tp_done[i - 2]] if i >= 2 else []), inc=pv)
        xn_done[i] = (pv, pv.v)
        if xntok16_ap is not None:
            do("vector", lambda e, p=p: e.tensor_copy(out=tok16[p][:, :], in_=xnt[p][:, :]),
               waits=[xn_done[i], (pv, pv.v)] + ([store_done[i - 2]] if i >= 2 else []), inc=pv)
            dma("gpsimd", lambda e, i=i, p=p: e.dma_start(
                out=xntok16_ap[i * 128:(i + 1) * 128, :], in_=tok16[p][:, :]), s_st[p],
                waits=[(pv, pv.v)])

        for c in range(nchunk):
            cp = c % 2
            tw = [xn_done[i]]
            prev = (i, c - 2) if c >= 2 else ((i - 1, nchunk - 2 + c) if i >= 1 else None)
            if prev is not None and prev in evac_done:
                tw.extend(evac_done[prev])
            if i == 0 and c == 0:
                tw.append(ident_done)
            do("tensor", lambda e, p=p, c=c, cp=cp: e.transpose(out=psT[cp][:, :],
                                                                in_=xnt[p][:, c * 128:(c + 1) * 128],
                                                                identity=idf[:, :]),
               waits=tw, inc=pt, tag=f"tp{i}_{c}")
            tp = (pt, pt.v)
            if c == nchunk - 1:
                tp_done[i] = tp
            sw = [tp] + ([store_done[i - 2]] if (i >= 2 and c == 0) else [])
            do("scalar", lambda e, p=p, c=c, cp=cp: e.copy(out=fm32[p][:, c * 128:(c + 1) * 128],
                                                           in_=psT[cp][:, :]),
               waits=sw, inc=ps_)
            do("vector", lambda e, p=p, c=c, cp=cp: e.tensor_copy(out=fm16[p][:, c * 128:(c + 1) * 128],
                                                                  in_=fm32[p][:, c * 128:(c + 1) * 128]),
               waits=[(ps_, ps_.v)] + ([store_done[i - 2], mml_done[i - 2]] if (i >= 2 and c == 0) else []), inc=pv)
            evac_done[(i, c)] = ((ps_, ps_.v), (pv, pv.v))

        lmw = [evac_done[(i, nchunk - 1)][1], wrT_done]
        if i >= 2:
            lmw.append((ps_, ps_.v))
        def mml(e, p=p):
            for c in range(nchunk):
                mm = e.matmul(out=psL[p][:, :], lhsT=fm16[p][:, c * 128:(c + 1) * 128],
                              rhs=wrT[:, c * 8:(c + 1) * 8], start=(c == 0), stop=(c == nchunk - 1))
            return mm
        do("tensor", mml, waits=lmw, inc=pt, tag=f"mml{i}")
        mml_done[i] = (pt, pt.v)
        do("scalar", lambda e, p=p: e.copy(out=lst[p][:, :], in_=psL[p][:, :]),
           waits=[(pt, pt.v)], inc=ps_)
        lcp = (ps_, ps_.v)

        for c in range(nchunk):
            dma("gpsimd", lambda e, p=p, c=c, i=i: e.dma_start(
                out=xnfm32_ap[c * 128:(c + 1) * 128, i * 128:(i + 1) * 128],
                in_=fm32[p][:, c * 128:(c + 1) * 128]),
                s_st[p], waits=[evac_done[(i, c)][0]], tag=f"st32_{i}_{c}")
            dma("gpsimd", lambda e, p=p, c=c, i=i: e.dma_start(
                out=xnfm16_ap[c * 128:(c + 1) * 128, i * 128:(i + 1) * 128],
                in_=fm16[p][:, c * 128:(c + 1) * 128]),
                s_st[p], waits=[evac_done[(i, c)][1]], tag=f"st16_{i}_{c}")
        store_done[i] = dma("gpsimd", lambda e, p=p, i=i: e.dma_start(
            out=logits_ap[i * 128:(i + 1) * 128, :], in_=lst[p][:, :]), s_st[p], waits=[lcp])
    do("gpsimd", None, waits=[(s_st[0], s_st[0].v), (s_st[1], s_st[1].v)])
    do("sync", None, waits=[(s_ldx[0], s_ldx[0].v), (s_ldx[1], s_ldx[1].v)])


def ph_l0_attnA(k: KB, T, cur32_ap, wq_ap, wk_ap, wv_ap, wo_ap, bq_ap, bk_ap,
                bv_ap, bo_ap, omega_ap, g1_ap, b1_ap, h1f32_ap, h1f16_ap, uniq="l0a", dbg=None, stage="full"):
    nc, do, dma = k.nc, k.do, k.dmago
    D, H, DH, NT = 1024, 16, 64, T // 512
    NSC = 4
    pv, ps_, pt, pg = k.pg["vector"], k.pg["scalar"], k.pg["tensor"], k.pg["gpsimd"]
    U = uniq

    wtmp = [k.sb(f"{U}_wtmp0", [128, 512], F32)] * 2
    w16 = {n: k.sb(f"{U}_w16{n}", [128, 8192], BF16) for n in "qkvo"}
    bcol = {n: k.sb(f"{U}_bc{n}", [128, 8], F32) for n in ("q", "k", "o", "g", "b")}
    bvrow = k.sb(f"{U}_bvrow", [1, 1024], F32)
    bvb = k.sb(f"{U}_bvb", [128, 1024], BF16)
    omg = k.sb(f"{U}_omg", [128, 32], F32)
    omg2 = k.sb(f"{U}_omg2", [128, 32], BF16)
    omg2n = k.sb(f"{U}_omg2n", [128, 32], BF16)
    cur32 = [k.sb(f"{U}_cur320", [128, 4096], F32)] * 2
    cur16 = k.sb(f"{U}_cur16", [128, 4096], BF16)
    q16 = k.sb(f"{U}_q16", [128, 4096], BF16)
    k16 = k.sb(f"{U}_k16", [128, 4096], BF16)
    qf16 = k.sb(f"{U}_qf16", [128, 4096], BF16)
    kf16 = k.sb(f"{U}_kf16", [128, 4096], BF16)
    sqs = k.sb(f"{U}_sqs", [128, 512], BF16)
    qfe = sqs
    ehq = k.sb(f"{U}_ehq", [16, 512], BF16)
    ehk = k.sb(f"{U}_ehk", [16, 512], BF16)
    kfts = k.sb(f"{U}_kfts", [128, 1024], BF16)
    v16 = k.sb(f"{U}_v16", [128, 1024], BF16)
    atok = k.sb(f"{U}_atok", [128, 1024], BF16)
    aT16 = k.sb(f"{U}_aT16", [128, 4096], BF16)
    asb = k.sb(f"{U}_asb", [128, 128], BF16)
    zq = [k.sb(f"{U}_zq{i}", [128, 128], BF16) for i in range(2)]
    zk = [k.sb(f"{U}_zk{i}", [128, 128], BF16) for i in range(2)]
    dsc = k.sb(f"{U}_dsc", [128, 1], F32)
    S32 = k.sb(f"{U}_S32", [128, 8 * 65], F32)
    S16 = k.sb(f"{U}_S16", [128, 8 * 65], BF16)

    def ssl(t, h, c0, c1):
        rb = (h % 2) * 64
        base = (h // 2) * 65
        return t[rb:rb + 64, base + c0:base + c1]
    r32 = k.sb(f"{U}_r32", [128, 4096], F32)
    aost = k.sb(f"{U}_aost", [128, 512], F32)
    sq32 = aost
    t1sb = aost
    h1st16 = k.sb(f"{U}_h1st16", [128, 4096], BF16)
    lnrow = k.sb(f"{U}_lnrow", [1, 512 * 4], F32)
    onesf = k.sb(f"{U}_onesf", [128, 1], F32)
    onesk1 = k.sb(f"{U}_onesk1", [1, 128], F32)

    hsel_np = np.zeros((128, 16 * 8), np.float32)
    selb_np = np.zeros((16, 128 * 8), np.float32)
    for cc in range(8):
        hsel_np[0:64, cc * 16 + 2 * cc] = 0.0625
        hsel_np[64:128, cc * 16 + 2 * cc + 1] = 0.0625
        selb_np[2 * cc, cc * 128 + 0:cc * 128 + 64] = 0.125
        selb_np[2 * cc + 1, cc * 128 + 64:cc * 128 + 128] = 0.125
    maskut_np = np.triu(np.ones((128, 128), np.float32))
    ident_np = np.eye(128, dtype=ml_dtypes.bfloat16)
    ones128_np = np.ones((128, 1), np.float32)
    c_hsel = k.const(f"{U}_hselc", hsel_np.astype(ml_dtypes.bfloat16))
    c_selb = k.const(f"{U}_selbc", selb_np.astype(ml_dtypes.bfloat16))
    c_mask = k.const(f"{U}_maskc", maskut_np)
    c_id = k.const(f"{U}_idc", ident_np)
    c_ones = k.const(f"{U}_onesc", ones128_np.astype(ml_dtypes.bfloat16))
    hsel = k.sb(f"{U}_hsel", [128, 16 * 8], BF16)
    selb = k.sb(f"{U}_selb", [16, 128 * 8], BF16)
    maskut = k.sb(f"{U}_maskut", [128, 128], F32)
    idb = k.sb(f"{U}_idb", [128, 128], BF16)
    ones16 = k.sb(f"{U}_ones16", [128, 1], BF16)

    F = [k.ps(f"{U}_F{i}", [128, 512], F32) for i in range(6)]
    B = [k.ps(f"{U}_B{i}", [128, 1024], BF16) for i in range(2)]

    s_w = k.sem(f"{U}_sw")
    s_b = k.sem(f"{U}_sb")
    s_cur = [k.sem(f"{U}_scur{p}") for p in range(2)]
    s_st = [k.sem(f"{U}_sst{p}") for p in range(2)]

    for (dst, src) in [(hsel, c_hsel), (selb, c_selb), (maskut, c_mask), (idb, c_id), (ones16, c_ones)]:
        dma("sync", lambda e, dst=dst, src=src: e.dma_start(out=dst[:, :], in_=src[:, :]), s_b)
    do("vector", lambda e: e.memset(onesf[:], 1.0), inc=pv)
    do("vector", lambda e: e.memset(onesk1[:], 1.0), waits=[pv], inc=pv)
    do("vector", lambda e: e.memset(S32[:], 0.0), waits=[pv], inc=pv)
    do("vector", lambda e: e.memset(S16[:], 0.0), waits=[pv], inc=pv)
    for zz in zq + zk:
        do("vector", lambda e, zz=zz: e.memset(zz[:], 0.0), waits=[pv], inc=pv)
    setup_g = (pv, pv.v)
    wload = {}
    wsrc = {"q": wq_ap, "k": wk_ap, "v": wv_ap, "o": wo_ap}
    ci = 0
    for n in "qkvo":
        for c in range(8):
            for hh in range(2):
                wwait = [wload[ci - 1]] if ci >= 1 else []
                lw = dma("sync", lambda e, n=n, c=c, hh=hh: e.dma_start(out=wtmp[0][:, :],
                         in_=wsrc[n][c * 128:(c + 1) * 128, hh * 512:(hh + 1) * 512]), s_w, waits=wwait)
                do("vector", lambda e, n=n, c=c, hh=hh: e.tensor_copy(
                   out=w16[n][:, c * 1024 + hh * 512:c * 1024 + (hh + 1) * 512],
                   in_=wtmp[0][:, :]), waits=[lw], inc=pv)
                wload[ci] = (pv, pv.v)
                ci += 1
    w_done = (pv, pv.v)
    bsrc = {"q": bq_ap, "k": bk_ap, "o": bo_ap, "g": g1_ap, "b": b1_ap}
    for n in bsrc:
        for c in range(8):
            dma("sync", lambda e, n=n, c=c: e.dma_start(out=bcol[n][:, c:c + 1],
                in_=bsrc[n][c * 128:(c + 1) * 128]), s_b)
    dma("sync", lambda e: e.dma_start(out=bvrow[:, :], in_=bv_ap[:].rearrange("(o d) -> o d", o=1)), s_b)
    dma("sync", lambda e: e.dma_start(out=omg[0:64, :], in_=omega_ap[:, :]), s_b)
    dma("sync", lambda e: e.dma_start(out=omg[64:128, :], in_=omega_ap[:, :]), s_b)
    allb_ld = (s_b, s_b.v)
    ld_consts = allb_ld
    bv_ld = allb_ld
    om_ld = allb_ld
    for half in range(2):
        do("tensor", lambda e, half=half: e.matmul(out=F[4][:, 0:512], lhsT=onesk1[:, :],
           rhs=bvrow[:, half * 512:(half + 1) * 512], start=True, stop=True),
           waits=[bv_ld, setup_g] + ([(pv, pv.v)] if half else []), inc=pt)
        do("vector", lambda e, half=half: e.tensor_copy(out=bvb[:, half * 512:(half + 1) * 512],
           in_=F[4][:, 0:512]), waits=[(pt, pt.v)], inc=pv)
    bias_done = [(pv, pv.v), (s_w, s_w.v)]
    do("vector", lambda e: e.tensor_scalar(out=omg2[:, :], in0=omg[:, :], scalar1=float(DH ** -0.25),
       scalar2=None, op0=ALU.mult), waits=[om_ld], inc=pv)
    do("vector", lambda e: e.tensor_scalar(out=omg2n[:, :], in0=omg2[:, :], scalar1=-1.0,
       scalar2=None, op0=ALU.mult), waits=[(pv, pv.v)], inc=pv)
    om_done = (pv, pv.v)

    LNEPS = 1e-5
    st_done = {}
    tile_done = {}
    B_free = {0: None, 1: None}
    s_cast_done = {}
    for tt in range(NT):
        p = tt % 2
        tw = [tile_done[tt - 1]] if tt >= 1 else []
        lc = None
        for c in range(8):
            lc = dma("sync", lambda e, tt=tt, c=c, p=p: e.dma_start(
                out=cur32[p][:, c * 512:(c + 1) * 512],
                in_=cur32_ap[c * 128:(c + 1) * 128, tt * 512:(tt + 1) * 512]), s_cur[p], waits=tw if c == 0 else [])
        cw = [lc, w_done] + ([tile_done[tt - 1]] if tt >= 1 else [])
        do("vector", lambda e, p=p: e.tensor_copy(out=cur16[:, :], in_=cur32[p][:, :]), waits=cw, inc=pv)
        cast_done = (pv, pv.v)
        s_cast_done[tt] = cast_done

        proj_done = {}
        for (n, dst) in (("q", q16), ("k", k16)):
            for f in range(8):
                fb = f % 2
                mmw = [cast_done]
                if (n, f) == ("q", 0):
                    mmw += bias_done
                def mmp(e, n=n, f=f, fb=fb):
                    for c in range(8):
                        mm = e.matmul(out=F[fb][:, :], lhsT=w16[n][:, c * 1024 + f * 128: c * 1024 + (f + 1) * 128],
                                      rhs=cur16[:, c * 512:(c + 1) * 512], start=(c == 0), stop=(c == 7))
                    return mm
                if f >= 2:
                    mmw.append(proj_done[(n, f - 2)])
                elif n == "k" and f < 2:
                    mmw.append(proj_done[("q", 6 + f)])
                do("tensor", mmp, waits=mmw, inc=pt)
                do("scalar", lambda e, n=n, f=f, fb=fb, dst=dst: e.activation(
                    out=dst[:, f * 512:(f + 1) * 512], in_=F[fb][:, :], func=AF.Identity, bias=bcol[n][:, f:f + 1]),
                    waits=[(pt, pt.v)], inc=ps_)
                proj_done[(n, f)] = (ps_, ps_.v)
        projq_done = proj_done[("q", 7)]
        projk_done = proj_done[("k", 7)]

        if dbg is not None:
            for (nm, buf, mark) in (("q16", q16, proj_done[("q", 7)]), ("k16", k16, proj_done[("k", 7)])):
                if nm in dbg:
                    for c in range(8):
                        dma("gpsimd", lambda e, nm=nm, buf=buf, c=c, tt=tt: e.dma_start(
                            out=dbg[nm][c * 128:(c + 1) * 128, tt * 512:(tt + 1) * 512],
                            in_=buf[:, c * 512:(c + 1) * 512]), s_st[p], waits=[mark])
        if stage == "proj":
            tile_done[tt] = (ps_, ps_.v)
            st_done[tt] = (s_st[p], s_st[p].v)
            continue
        hq_mm = hk_mm = None
        for (n, src, bank) in (("q", q16, 2), ("k", k16, 3)):
            for cc in range(8):
                aw = [proj_done[(n, 7)], ld_consts]
                if cc == 0 and tt >= 1:
                    aw.append(tile_done[tt - 1])
                do("scalar", lambda e, src=src, cc=cc: e.activation(out=sqs[:, :],
                   in_=src[:, cc * 512:(cc + 1) * 512], func=AF.Square),
                   waits=aw + [(pt, pt.v)], inc=ps_)
                do("tensor", lambda e, cc=cc, bank=bank: e.matmul(out=F[bank][0:16, :],
                   lhsT=hsel[:, cc * 16:(cc + 1) * 16], rhs=sqs[:, :], start=(cc == 0), stop=(cc == 7)),
                   waits=[(ps_, ps_.v)], inc=pt)
            if n == "q":
                hq_mm = (pt, pt.v)
            else:
                hk_mm = (pt, pt.v)
        do("scalar", lambda e: e.activation(out=ehq[:, :], in_=F[2][0:16, :], func=AF.Exp, scale=-1.0),
           waits=[hq_mm], inc=ps_)
        ehq_done = (ps_, ps_.v)
        do("scalar", lambda e: e.activation(out=ehk[:, :], in_=F[3][0:16, :], func=AF.Exp, scale=-1.0),
           waits=[hk_mm], inc=ps_)
        ehk_done = (ps_, ps_.v)

        feat_done = {}
        for (n, src, eh, ehd, dst) in (("q", q16, ehq, ehq_done, qf16), ("k", k16, ehk, ehk_done, kf16)):
            for cc in range(8):
                ub = cc % 2
                eb = 4 + cc % 2
                uw = [proj_done[(n, 7)], om_done]
                if (n, cc) in (("q", 0), ("q", 1)):
                    uw.append(ehk_done)
                prev = feat_done.get((n, cc - 2)) or (feat_done.get(("q", 6 + cc)) if n == "k" else None)
                if prev:
                    uw.append(prev)
                def mmu(e, src=src, cc=cc, ub=ub):
                    e.matmul(out=F[ub][0:32, :], lhsT=omg2[0:64, :], rhs=src[0:64, cc * 512:(cc + 1) * 512],
                             start=True, stop=True, tile_position=(0, 0))
                    e.matmul(out=F[ub][32:64, :], lhsT=omg2n[0:64, :], rhs=src[0:64, cc * 512:(cc + 1) * 512],
                             start=True, stop=True, tile_position=(0, 32))
                    e.matmul(out=F[ub][64:96, :], lhsT=omg2[64:128, :], rhs=src[64:128, cc * 512:(cc + 1) * 512],
                             start=True, stop=True, tile_position=(64, 64))
                    return e.matmul(out=F[ub][96:128, :], lhsT=omg2n[64:128, :],
                                    rhs=src[64:128, cc * 512:(cc + 1) * 512],
                                    start=True, stop=True, tile_position=(64, 96))
                do("tensor", mmu, waits=uw, inc=pt)
                umm = (pt, pt.v)
                emw = [ehd]
                if prev:
                    emw.append(prev)
                elif n == "q":
                    emw.extend(bias_done)
                do("tensor", lambda e, eh=eh, cc=cc, eb=eb: e.matmul(out=F[eb][:, :],
                   lhsT=selb[:, cc * 128:(cc + 1) * 128], rhs=eh[:, :], start=True, stop=True),
                   waits=emw, inc=pt)
                emm = (pt, pt.v)
                do("scalar", lambda e, cc=cc, ub=ub: e.activation(out=qfe[:, :], in_=F[ub][:, :], func=AF.Exp),
                   waits=[umm] + ([(pv, pv.v)] if True else []), inc=ps_)
                do("vector", lambda e, dst=dst, cc=cc, eb=eb: e.tensor_tensor(out=dst[:, cc * 512:(cc + 1) * 512],
                   in0=qfe[:, :], in1=F[eb][:, :], op=ALU.mult), waits=[(ps_, ps_.v), emm], inc=pv)
                feat_done[(n, cc)] = (pv, pv.v)
        qf_done = feat_done[("q", 7)]
        kf_done = feat_done[("k", 7)]
        if dbg is not None:
            for (nm, buf, mark) in (("qf16", qf16, qf_done), ("kf16", kf16, kf_done)):
                if nm in dbg:
                    for c in range(8):
                        dma("gpsimd", lambda e, nm=nm, buf=buf, c=c, tt=tt: e.dma_start(
                            out=dbg[nm][c * 128:(c + 1) * 128, tt * 512:(tt + 1) * 512],
                            in_=buf[:, c * 512:(c + 1) * 512]), s_st[p], waits=[mark])

        if stage == "favor":
            tile_done[tt] = (pv, pv.v)
            st_done[tt] = (s_st[p], s_st[p].v)
            continue
        sc_mark = {}
        NSC_eff = NSC
        for sc in range(NSC_eff):
            col = sc * 128
            for half in range(2):
                vb = half % 2
                vw = [cast_done]
                if sc == 0 and half == 0:
                    vw += [qf_done, kf_done]
                if (sc, half) != (0, 0):
                    vw.append(sc_mark.get((sc, "v", half - 1)) or sc_mark.get((sc - 1, "vlast")) or (pv, pv.v))
                def mmv(e, sc=sc, half=half, vb=vb, col=col):
                    for c in range(8):
                        mm = e.matmul(out=F[vb][:, :],
                                      lhsT=cur16[:, c * 512 + col:c * 512 + col + 128],
                                      rhs=w16["v"][:, c * 1024 + half * 512: c * 1024 + (half + 1) * 512],
                                      start=(c == 0), stop=(c == 7))
                    return mm
                do("tensor", mmv, waits=vw, inc=pt)
                do("vector", lambda e, half=half, vb=vb: e.scalar_tensor_tensor(
                    out=v16[:, half * 512:(half + 1) * 512], in0=F[vb][:, :], scalar=1.0,
                    in1=bvb[:, half * 512:(half + 1) * 512], op0=ALU.mult, op1=ALU.add),
                    waits=[(pt, pt.v)], inc=pv)
                sc_mark[(sc, "v", half)] = (pv, pv.v)
            v_done = (pv, pv.v)
            sc_mark[(sc, "vlast")] = v_done
            if stage == "scanv":
                continue
            kft_ev = {}
            for grp in range(2):
                def mtp(e, grp=grp, sc=sc, col=col):
                    for j in range(4):
                        cc = grp * 4 + j
                        mm = e.transpose(out=B[grp][:, j * 128:(j + 1) * 128],
                                         in_=kf16[:, cc * 512 + col: cc * 512 + col + 128],
                                         identity=idb[:, :])
                    return mm
                tpw = [kf_done, ld_consts]
                if B_free[grp] is not None:
                    tpw.append(B_free[grp])
                do("tensor", mtp, waits=tpw, inc=pt)
                do("scalar", lambda e, grp=grp: e.copy(out=kfts[:, grp * 512:(grp + 1) * 512],
                   in_=B[grp][:, 0:512]), waits=[(pt, pt.v)], inc=ps_)
                sc_mark[(sc, "kft", grp)] = (ps_, ps_.v)
                B_free[grp] = (ps_, ps_.v)
            kft_done = (ps_, ps_.v)
            if stage in ("scanv", "scankft"):
                continue
            for h in range(H):
                cc, rb = h // 2, (h % 2) * 64
                it = sc * H + h
                Ab = F[it % 2]
                Nb = F[2 + it % 2]
                Sb = F[4 + it % 2]
                par = h % 2
                zw = [qf_done, kf_done]
                if it >= 2:
                    zw.append(sc_mark[("zfree", it - 2)])
                do("vector", lambda e, cc=cc, rb=rb, par=par, col=col: e.tensor_copy(
                   out=zq[par][rb:rb + 64, :], in_=qf16[rb:rb + 64, cc * 512 + col: cc * 512 + col + 128]),
                   waits=zw, inc=pv)
                do("vector", lambda e, cc=cc, rb=rb, par=par, col=col: e.tensor_copy(
                   out=zk[par][rb:rb + 64, :], in_=kf16[rb:rb + 64, cc * 512 + col: cc * 512 + col + 128]),
                   waits=[(pv, pv.v)], inc=pv)
                z_done = (pv, pv.v)
                aw = [z_done, setup_g]
                if it >= 2:
                    aw.append(sc_mark[("a16", it - 2)])
                do("tensor", lambda e, cc=cc, par=par, col=col, Ab=Ab: e.matmul(out=Ab[:, 0:128],
                   lhsT=zk[par][:, :],
                   rhs=qf16[:, cc * 512 + col: cc * 512 + col + 128],
                   start=True, stop=True), waits=aw, inc=pt)
                do("vector", lambda e, Ab=Ab: e.tensor_tensor(out=asb[:, :], in0=Ab[:, 0:128],
                   in1=maskut[:, :], op=ALU.mult), waits=[(pt, pt.v)], inc=pv)
                a16_done = (pv, pv.v)
                sc_mark[("a16", it)] = a16_done
                if stage == "scanA":
                    sc_mark[("atok", it)] = a16_done
                    sc_mark[("scast", it)] = a16_done
                    continue
                nw = [a16_done, v_done]
                if it >= 2:
                    nw.append(sc_mark[("atok", it - 2)])
                if sc == 0 and h < 2:
                    nw.append((pv, pv.v))
                hb = (h // 2) * 65
                def mmn(e, h=h, hb=hb, par=par, Nb=Nb):
                    e.matmul(out=Nb[:, 0:64], lhsT=zq[par][:, :],
                             rhs=S16[:, hb:hb + 64], start=True, stop=False)
                    e.matmul(out=Nb[:, 0:64], lhsT=asb[:, :], rhs=v16[:, h * 64:(h + 1) * 64],
                             start=False, stop=True)
                    e.matmul(out=Nb[:, 64:65], lhsT=zq[par][:, :],
                             rhs=S16[:, hb + 64:hb + 65], start=True, stop=False)
                    return e.matmul(out=Nb[:, 64:65], lhsT=asb[:, :], rhs=ones16[:, :],
                                    start=False, stop=True)
                do("tensor", mmn, waits=nw, inc=pt)
                nd_done = (pt, pt.v)
                sc_mark[("zfree", it)] = nd_done
                do("vector", lambda e, Nb=Nb: e.tensor_scalar(out=dsc[:, :], in0=Nb[:, 64:65],
                   scalar1=1e-6, scalar2=None, op0=ALU.add), waits=[nd_done], inc=pv)
                do("vector", lambda e: e.reciprocal(out=dsc[:, :], in_=dsc[:, :]),
                   waits=[(pv, pv.v)], inc=pv)
                atw = [(pv, pv.v)]
                if h == 0 and sc > 0:
                    atw += [sc_mark[(sc - 1, "mta", 0)], sc_mark[(sc - 1, "mta", 1)]]
                do("vector", lambda e, h=h, Nb=Nb: e.tensor_scalar(out=atok[:, h * 64:(h + 1) * 64],
                   in0=Nb[:, 0:64], scalar1=dsc[:, 0:1], scalar2=None, op0=ALU.mult),
                   waits=atw, inc=pv)
                sc_mark[("atok", it)] = (pv, pv.v)
                if stage == "scanN":
                    sc_mark[("scast", it)] = (pv, pv.v)
                    continue
                suw = [kft_done, v_done, nd_done]
                if it >= 2:
                    suw.append(sc_mark[("scast", it - 2)])
                srb = (h % 2) * 64
                def mms(e, h=h, Sb=Sb, srb=srb):
                    e.matmul(out=Sb[srb:srb + 64, 0:64], lhsT=kfts[:, h * 64:(h + 1) * 64],
                             rhs=v16[:, h * 64:(h + 1) * 64], start=True, stop=False,
                             tile_position=(0, srb))
                    return e.matmul(out=Sb[srb:srb + 64, 64:65], lhsT=kfts[:, h * 64:(h + 1) * 64],
                                    rhs=ones16[:, :], start=False, stop=True, tile_position=(0, srb))
                do("tensor", mms, waits=suw, inc=pt)
                do("vector", lambda e, h=h, Sb=Sb, srb=srb: e.tensor_add(out=ssl(S32, h, 0, 65),
                   in0=ssl(S32, h, 0, 65), in1=Sb[srb:srb + 64, 0:65]), waits=[(pt, pt.v)], inc=pv)
                do("vector", lambda e, h=h: e.tensor_copy(out=ssl(S16, h, 0, 65),
                   in_=ssl(S32, h, 0, 65)), waits=[(pv, pv.v)], inc=pv)
                sc_mark[("scast", it)] = (pv, pv.v)
            if stage in ("scanA",):
                continue
            at_done = sc_mark[("atok", sc * H + H - 1)]
            for grp in range(2):
                def mta(e, grp=grp, sc=sc):
                    for j in range(4):
                        c = grp * 4 + j
                        mm = e.transpose(out=B[grp][:, j * 128:(j + 1) * 128],
                                         in_=atok[:, c * 128:(c + 1) * 128], identity=idb[:, :])
                    return mm
                taw = [at_done, B_free[grp]]
                do("tensor", mta, waits=taw, inc=pt)
                tadone = (pt, pt.v)
                sc_mark[(sc, "mta", grp)] = tadone
                for j in range(4):
                    c = grp * 4 + j
                    do("scalar", lambda e, grp=grp, j=j, c=c, sc=sc, col=col: e.copy(
                        out=aT16[:, c * 512 + col: c * 512 + col + 128],
                        in_=B[grp][:, j * 128:(j + 1) * 128]), waits=[tadone], inc=ps_)
                sc_mark[(sc, "aev", grp)] = (ps_, ps_.v)
                B_free[grp] = (ps_, ps_.v)
        aT_done = (ps_, ps_.v)
        if dbg is not None and "aT16" in dbg:
            for c in range(8):
                dma("gpsimd", lambda e, c=c, tt=tt: e.dma_start(
                    out=dbg["aT16"][c * 128:(c + 1) * 128, tt * 512:(tt + 1) * 512],
                    in_=aT16[:, c * 512:(c + 1) * 512]), s_st[p], waits=[aT_done])

        if stage.startswith("scan"):
            tile_done[tt] = (ps_, ps_.v) if stage == "scan" else (pv, pv.v)
            st_done[tt] = (s_st[p], s_st[p].v)
            continue
        wo_ev = {}
        for f in range(8):
            fb = f % 2
            ww = [aT_done]
            if f >= 2:
                ww.append(wo_ev[f - 2])
            def mmo(e, f=f, fb=fb):
                for c in range(8):
                    mm = e.matmul(out=F[fb][:, :], lhsT=w16["o"][:, c * 1024 + f * 128: c * 1024 + (f + 1) * 128],
                                  rhs=aT16[:, c * 512:(c + 1) * 512], start=(c == 0), stop=(c == 7))
                return mm
            do("tensor", mmo, waits=ww, inc=pt)
            evw = [(pt, pt.v), (ps_, ps_.v)]
            if f >= 1:
                evw.append(wo_ev[f - 1])
            do("scalar", lambda e, f=f, fb=fb: e.activation(out=aost[:, :], in_=F[fb][:, :],
               func=AF.Identity, bias=bcol["o"][:, f:f + 1]), waits=evw, inc=ps_)
            ao_done = (ps_, ps_.v)
            rw = [ao_done]
            if f == 0 and tt >= 1:
                rw.append(st_done[tt - 1])
            do("vector", lambda e, f=f, p=p: e.tensor_add(out=r32[:, f * 512:(f + 1) * 512],
               in0=aost[:, :], in1=cur32[p][:, f * 512:(f + 1) * 512]), waits=rw, inc=pv)
            wo_ev[f] = (pv, pv.v)
        r_done = (pv, pv.v)
        if dbg is not None and "r32" in dbg:
            for c in range(8):
                dma("gpsimd", lambda e, c=c, tt=tt: e.dma_start(
                    out=dbg["r32"][c * 128:(c + 1) * 128, tt * 512:(tt + 1) * 512],
                    in_=r32[:, c * 512:(c + 1) * 512]), s_st[p], waits=[r_done])

        def mm_sum(e):
            for c in range(8):
                mm = e.matmul(out=F[2][0:1, :], lhsT=onesf[:, :], rhs=r32[:, c * 512:(c + 1) * 512],
                              start=(c == 0), stop=(c == 7))
            return mm
        do("tensor", mm_sum, waits=[r_done], inc=pt)
        sum_done = (pt, pt.v)
        sq_ev = {}
        for c in range(8):
            do("scalar", lambda e, c=c: e.activation(out=sq32[:, :], in_=r32[:, c * 512:(c + 1) * 512],
               func=AF.Square), waits=[(pv, pv.v) if c == 0 else sq_ev[c - 1], r_done] + ([(pt, pt.v)] if c == 0 else []), inc=ps_)
            do("tensor", lambda e, c=c: e.matmul(out=F[3][0:1, :], lhsT=onesf[:, :], rhs=sq32[:, :],
               start=(c == 0), stop=(c == 7)), waits=[(ps_, ps_.v)], inc=pt)
            sq_ev[c] = (pt, pt.v)
        ssq_done = (pt, pt.v)
        do("vector", lambda e: e.tensor_scalar(out=lnrow[:, 0:512], in0=F[2][0:1, :], scalar1=1.0 / D,
           scalar2=None, op0=ALU.mult), waits=[sum_done], inc=pv)
        do("vector", lambda e: e.tensor_scalar(out=lnrow[:, 512:1024], in0=F[3][0:1, :], scalar1=1.0 / D,
           scalar2=None, op0=ALU.mult), waits=[ssq_done, (pv, pv.v)], inc=pv)
        do("vector", lambda e: e.scalar_tensor_tensor(out=lnrow[:, 1024:1536], in0=lnrow[:, 0:512],
           scalar=1.0, in1=lnrow[:, 0:512], op0=ALU.mult, op1=ALU.mult), waits=[(pv, pv.v)], inc=pv)
        do("vector", lambda e: e.tensor_sub(out=lnrow[:, 512:1024], in0=lnrow[:, 512:1024],
           in1=lnrow[:, 1024:1536]), waits=[(pv, pv.v)], inc=pv)
        do("vector", lambda e: e.tensor_scalar(out=lnrow[:, 512:1024], in0=lnrow[:, 512:1024],
           scalar1=LNEPS, scalar2=None, op0=ALU.add), waits=[(pv, pv.v)], inc=pv)
        do("scalar", lambda e: e.activation(out=lnrow[:, 1024:1536], in_=lnrow[:, 512:1024],
           func=AF.Sqrt), waits=[(pv, pv.v)], inc=ps_)
        do("vector", lambda e: e.reciprocal(out=lnrow[:, 1536:2048], in_=lnrow[:, 1024:1536]),
           waits=[(ps_, ps_.v)], inc=pv)
        ln_stats = (pv, pv.v)
        do("tensor", lambda e: e.matmul(out=F[4][:, :], lhsT=onesk1[:, :], rhs=lnrow[:, 0:512],
           start=True, stop=True), waits=[ln_stats], inc=pt)
        mb_done = (pt, pt.v)
        do("tensor", lambda e: e.matmul(out=F[5][:, :], lhsT=onesk1[:, :], rhs=lnrow[:, 1536:2048],
           start=True, stop=True), waits=[ln_stats], inc=pt)
        rb_done = (pt, pt.v)
        stw = [st_done[tt - 2]] if tt >= 2 else []
        if dbg is not None:
            stw.append((s_st[p], s_st[p].v))
        h1_ev = {}
        for c in range(8):
            do("vector", lambda e, c=c: e.tensor_sub(out=t1sb[:, :], in0=r32[:, c * 512:(c + 1) * 512],
               in1=F[4][:, :]), waits=[mb_done, (pv, pv.v)], inc=pv)
            do("vector", lambda e, c=c: e.tensor_tensor(out=t1sb[:, :], in0=t1sb[:, :], in1=F[5][:, :],
               op=ALU.mult), waits=[rb_done, (pv, pv.v)], inc=pv)
            do("vector", lambda e, c=c: e.tensor_scalar(out=r32[:, c * 512:(c + 1) * 512], in0=t1sb[:, :],
               scalar1=bcol["g"][:, c:c + 1], scalar2=bcol["b"][:, c:c + 1], op0=ALU.mult, op1=ALU.add),
               waits=[(pv, pv.v)] + stw, inc=pv)
            do("vector", lambda e, c=c: e.tensor_copy(out=h1st16[:, c * 512:(c + 1) * 512],
               in_=r32[:, c * 512:(c + 1) * 512]), waits=[(pv, pv.v)], inc=pv)
            h1_ev[c] = (pv, pv.v)
            dma("scalar", lambda e, c=c, tt=tt: e.dma_start(
                out=h1f32_ap[c * 128:(c + 1) * 128, tt * 512:(tt + 1) * 512],
                in_=r32[:, c * 512:(c + 1) * 512]), s_st[p],
                waits=[h1_ev[c]], tag=f"h1st32_{tt}_{c}")
        st_done[tt] = (s_st[p], s_st[p].v)
        tile_done[tt] = (pv, pv.v)
    do("gpsimd", None, waits=[(s_st[0], s_st[0].v), (s_st[1], s_st[1].v)])


def ph_l0_ffnB(k: KB, T, h1f32_ap, w1_ap, w2_ap, bf1_ap, bf2_ap, g2_ap, b2_ap,
               curo32_ap, uniq="l0b"):
    nc, do, dma = k.nc, k.do, k.dmago
    D, DFF = 1024, 4096
    TT = 256
    NT = T // TT
    pv, ps_, pt, pg = k.pg["vector"], k.pg["scalar"], k.pg["tensor"], k.pg["gpsimd"]
    U = uniq

    w116 = k.sb(f"{U}_w116", [128, 8 * DFF], BF16)
    w216 = k.sb(f"{U}_w216", [128, 32 * D], BF16)
    bcol1 = k.sb(f"{U}_bc1", [128, 32], F32)
    bcol2 = k.sb(f"{U}_bc2", [128, 8], F32)
    gcol = k.sb(f"{U}_gc", [128, 8], F32)
    bbcol = k.sb(f"{U}_bb", [128, 8], F32)
    h32 = [k.sb(f"{U}_h32{p}", [128, 8 * TT], F32) for p in range(2)]
    h16 = [k.sb(f"{U}_h16{p}", [128, 8 * TT], BF16) for p in range(2)]
    y116 = [k.sb(f"{U}_y116{p}", [128, 32 * TT], BF16) for p in range(2)]
    y2st = k.sb(f"{U}_y2st", [128, TT], F32)
    r2 = k.sb(f"{U}_r2", [128, 8 * TT], F32)
    t1sb = y2st
    sqsc = y2st
    lnrow = k.sb(f"{U}_lnrow", [1, TT * 4], F32)
    onesf = k.sb(f"{U}_onesf", [128, 1], F32)
    onesk1 = k.sb(f"{U}_onesk1", [1, 128], F32)
    F = [k.ps(f"{U}_F{i}", [128, 512], F32) for i in range(6)]

    s_w = k.sem(f"{U}_sw")
    s_w1 = k.sem(f"{U}_sw1")
    s_b = k.sem(f"{U}_sb")
    s_h = [k.sem(f"{U}_sh{p}") for p in range(2)]
    s_st = [k.sem(f"{U}_sst{p}") for p in range(2)]

    do("vector", lambda e: e.memset(onesf[:], 1.0), inc=pv)
    do("vector", lambda e: e.memset(onesk1[:], 1.0), waits=[pv], inc=pv)
    ones_done = (pv, pv.v)
    w1_done = None
    for c in range(8):
        w1_done = dma("gpsimd", lambda e, c=c: e.dma_start(out=w116[:, c * DFF:(c + 1) * DFF],
                      in_=w1_ap[c * 128:(c + 1) * 128, :]), s_w1)
    w2_done = None
    for c in range(32):
        w2_done = dma("gpsimd", lambda e, c=c: e.dma_start(out=w216[:, c * D:(c + 1) * D],
                      in_=w2_ap[c * 128:(c + 1) * 128, :]), s_w)
    for c in range(32):
        dma("sync", lambda e, c=c: e.dma_start(out=bcol1[:, c:c + 1],
            in_=bf1_ap[c * 128:(c + 1) * 128]), s_b)
    for (dst, src) in ((bcol2, bf2_ap), (gcol, g2_ap), (bbcol, b2_ap)):
        for c in range(8):
            dma("sync", lambda e, dst=dst, src=src, c=c: e.dma_start(out=dst[:, c:c + 1],
                in_=src[c * 128:(c + 1) * 128]), s_b)
    b_done = (s_b, s_b.v)

    LNEPS = 1e-5
    st_done = {}
    tile_done = {}
    w1ev_prev = None
    y116_rd = {}
    for tt in range(NT):
        p = tt % 2
        tw = [tile_done[tt - 2]] if tt >= 2 else []
        lh = None
        for c in range(8):
            lh = dma("sync", lambda e, tt=tt, c=c, p=p: e.dma_start(
                out=h32[p][:, c * TT:(c + 1) * TT],
                in_=h1f32_ap[c * 128:(c + 1) * 128, tt * TT:(tt + 1) * TT]), s_h[p],
                waits=tw if c == 0 else [])
        h_ready = lh
        do("gpsimd", lambda e, p=p: e.tensor_copy(out=h16[p][:, :], in_=h32[p][:, :]),
           waits=[lh], inc=pg)
        lh16 = (pg, pg.v)
        ev = {}
        for ff in range(32):
            fb = ff % 2
            mw = [lh16, w1_done]
            if ff >= 2:
                mw.append(ev[ff - 2])
            elif w1ev_prev is not None:
                mw.append(w1ev_prev)
            def mm1(e, ff=ff, fb=fb, p=p):
                for c in range(8):
                    mm = e.matmul(out=F[fb][:, 0:TT], lhsT=w116[:, c * DFF + ff * 128: c * DFF + (ff + 1) * 128],
                                  rhs=h16[p][:, c * TT:(c + 1) * TT], start=(c == 0), stop=(c == 7))
                return mm
            do("tensor", mm1, waits=mw, inc=pt)
            evw = [(pt, pt.v), b_done]
            if ff < 2 and p in y116_rd:
                evw.append(y116_rd[p])
            do("scalar", lambda e, ff=ff, fb=fb, p=p: e.activation(out=y116[p][:, ff * TT:(ff + 1) * TT],
               in_=F[fb][:, 0:TT], func=AF.Relu, bias=bcol1[:, ff:ff + 1]), waits=evw, inc=ps_)
            ev[ff] = (ps_, ps_.v)
        y1_done = (ps_, ps_.v)
        w1ev_prev = (pt, pt.v)
        ev2 = {}
        for do_ in range(8):
            fb = 2 + do_ % 2
            mw = [y1_done, w2_done]
            if do_ >= 2:
                mw.append(ev2[do_ - 2])
            def mm2(e, do_=do_, fb=fb, p=p):
                for c in range(32):
                    mm = e.matmul(out=F[fb][:, 0:TT], lhsT=w216[:, c * D + do_ * 128: c * D + (do_ + 1) * 128],
                                  rhs=y116[p][:, c * TT:(c + 1) * TT], start=(c == 0), stop=(c == 31))
                return mm
            do("tensor", mm2, waits=mw, inc=pt)
            if do_ == 7:
                y116_rd[p] = (pt, pt.v)
            evw = [(pt, pt.v), (ps_, ps_.v)]
            if do_ >= 1:
                evw.append(ev2[do_ - 1])
            do("scalar", lambda e, do_=do_, fb=fb: e.activation(out=y2st[:, :], in_=F[fb][:, 0:TT],
               func=AF.Identity, bias=bcol2[:, do_:do_ + 1]), waits=evw, inc=ps_)
            rw = [(ps_, ps_.v)]
            if do_ == 0:
                rw.append(h_ready)
            if do_ == 0 and tt >= 1:
                rw.append(st_done[tt - 1])
            do("vector", lambda e, do_=do_, p=p: e.tensor_add(out=r2[:, do_ * TT:(do_ + 1) * TT],
               in0=y2st[:, :], in1=h32[p][:, do_ * TT:(do_ + 1) * TT]), waits=rw, inc=pv)
            ev2[do_] = (pv, pv.v)
        r_done = (pv, pv.v)
        def mm_sum(e):
            for c in range(8):
                mm = e.matmul(out=F[4][0:1, 0:TT], lhsT=onesf[:, :], rhs=r2[:, c * TT:(c + 1) * TT],
                              start=(c == 0), stop=(c == 7))
            return mm
        do("tensor", mm_sum, waits=[r_done, ones_done], inc=pt)
        sum_done = (pt, pt.v)
        sqev = None
        for c in range(8):
            sw = [r_done]
            if sqev:
                sw.append(sqev)
            do("scalar", lambda e, c=c: e.activation(out=sqsc[:, :], in_=r2[:, c * TT:(c + 1) * TT],
               func=AF.Square), waits=sw + ([(pt, pt.v)] if c == 0 else []), inc=ps_)
            do("tensor", lambda e, c=c: e.matmul(out=F[5][0:1, 0:TT], lhsT=onesf[:, :], rhs=sqsc[:, :],
               start=(c == 0), stop=(c == 7)), waits=[(ps_, ps_.v)], inc=pt)
            sqev = (pt, pt.v)
        ssq_done = (pt, pt.v)
        do("vector", lambda e: e.tensor_scalar(out=lnrow[:, 0:TT], in0=F[4][0:1, 0:TT], scalar1=1.0 / D,
           scalar2=None, op0=ALU.mult), waits=[sum_done], inc=pv)
        do("vector", lambda e: e.tensor_scalar(out=lnrow[:, TT:2 * TT], in0=F[5][0:1, 0:TT], scalar1=1.0 / D,
           scalar2=None, op0=ALU.mult), waits=[ssq_done, (pv, pv.v)], inc=pv)
        do("vector", lambda e: e.scalar_tensor_tensor(out=lnrow[:, 2 * TT:3 * TT], in0=lnrow[:, 0:TT],
           scalar=1.0, in1=lnrow[:, 0:TT], op0=ALU.mult, op1=ALU.mult), waits=[(pv, pv.v)], inc=pv)
        do("vector", lambda e: e.tensor_sub(out=lnrow[:, TT:2 * TT], in0=lnrow[:, TT:2 * TT],
           in1=lnrow[:, 2 * TT:3 * TT]), waits=[(pv, pv.v)], inc=pv)
        do("vector", lambda e: e.tensor_scalar(out=lnrow[:, TT:2 * TT], in0=lnrow[:, TT:2 * TT],
           scalar1=LNEPS, scalar2=None, op0=ALU.add), waits=[(pv, pv.v)], inc=pv)
        do("scalar", lambda e: e.activation(out=lnrow[:, 2 * TT:3 * TT], in_=lnrow[:, TT:2 * TT],
           func=AF.Sqrt), waits=[(pv, pv.v)], inc=ps_)
        do("vector", lambda e: e.reciprocal(out=lnrow[:, 3 * TT:4 * TT], in_=lnrow[:, 2 * TT:3 * TT]),
           waits=[(ps_, ps_.v)], inc=pv)
        ln_stats = (pv, pv.v)
        do("tensor", lambda e: e.matmul(out=F[4][:, 0:TT], lhsT=onesk1[:, :], rhs=lnrow[:, 0:TT],
           start=True, stop=True), waits=[ln_stats], inc=pt)
        mb_done = (pt, pt.v)
        do("tensor", lambda e: e.matmul(out=F[5][:, 0:TT], lhsT=onesk1[:, :], rhs=lnrow[:, 3 * TT:4 * TT],
           start=True, stop=True), waits=[ln_stats], inc=pt)
        rb_done = (pt, pt.v)
        stw = [st_done[tt - 2]] if tt >= 2 else []
        for c in range(8):
            do("vector", lambda e, c=c: e.tensor_sub(out=t1sb[:, :], in0=r2[:, c * TT:(c + 1) * TT],
               in1=F[4][:, 0:TT]), waits=[mb_done, (pv, pv.v)], inc=pv)
            do("vector", lambda e, c=c: e.tensor_tensor(out=t1sb[:, :], in0=t1sb[:, :], in1=F[5][:, 0:TT],
               op=ALU.mult), waits=[rb_done, (pv, pv.v)], inc=pv)
            do("vector", lambda e, c=c: e.tensor_scalar(out=r2[:, c * TT:(c + 1) * TT], in0=t1sb[:, :],
               scalar1=gcol[:, c:c + 1], scalar2=bbcol[:, c:c + 1], op0=ALU.mult, op1=ALU.add),
               waits=[(pv, pv.v)] + stw, inc=pv)
            cdone = (pv, pv.v)
            dma("scalar", lambda e, c=c, tt=tt: e.dma_start(
                out=curo32_ap[c * 128:(c + 1) * 128, tt * TT:(tt + 1) * TT],
                in_=r2[:, c * TT:(c + 1) * TT]), s_st[p], waits=[cdone])
        st_done[tt] = (s_st[p], s_st[p].v)
        tile_done[tt] = (pv, pv.v)
    do("gpsimd", None, waits=[(s_st[0], s_st[0].v), (s_st[1], s_st[1].v)])


def ph_moe_route(k: KB, Tg, logits_ap, ge_ap, fb_ap, cap, uniq="mr"):
    nc, do, dma = k.nc, k.do, k.dmago
    NC = Tg // 128
    pv, ps_, pt, pg = k.pg["vector"], k.pg["scalar"], k.pg["tensor"], k.pg["gpsimd"]
    U = uniq
    BIG = 1e30

    lg = k.sb(f"{U}_lg", [128, NC * 8], F32)
    m1 = k.sb(f"{U}_m1", [128, NC], F32)
    m2 = k.sb(f"{U}_m2", [128, NC], F32)
    tmp = k.sb(f"{U}_tmp", [128, NC], F32)
    den = k.sb(f"{U}_den", [128, NC], F32)
    we = k.sb(f"{U}_we", [128, NC * 8], F32)
    fb = k.sb(f"{U}_fb", [128, NC], F32)
    nf = k.sb(f"{U}_nf", [128, NC], F32)
    elig = k.sb(f"{U}_elig", [128, NC], F32)
    elig16 = k.sb(f"{U}_elig16", [128, NC], BF16)
    kept = k.sb(f"{U}_kept", [128, NC], F32)
    ge = k.sb(f"{U}_ge", [128, NC * 8], F32)
    rowA = k.sb(f"{U}_rowA", [1, NC], F32)
    rowB = k.sb(f"{U}_rowB", [1, NC], F32)
    row16 = k.sb(f"{U}_row16", [1, NC], BF16)
    triu16 = k.sb(f"{U}_triu16", [128, 128], BF16)
    ones16 = k.sb(f"{U}_ones16b", [128, 1], BF16)
    onesk1b = k.sb(f"{U}_onesk1b", [1, 128], BF16)
    onesk1f = k.sb(f"{U}_onesk1f", [1, 128], F32)
    P0 = k.ps(f"{U}_P0", [128, 512], F32)
    P1 = k.ps(f"{U}_P1", [128, 512], F32)

    c_triu = k.const(f"{U}_triuc", np.triu(np.ones((128, 128), np.float32)).astype(ml_dtypes.bfloat16))
    s_ld = k.sem(f"{U}_sld")
    s_st = k.sem(f"{U}_sst")

    def lge(t, e):
        return t[:, e::8]

    dma("sync", lambda e: e.dma_start(out=lg[:, :].rearrange("p (c e) -> p c e", e=8),
        in_=logits_ap.rearrange("(c p) e -> p c e", p=128)), s_ld)
    dma("sync", lambda e: e.dma_start(out=triu16[:, :], in_=c_triu[:, :]), s_ld)
    do("gpsimd", lambda e: e.memset(ones16[:], 1.0), inc=pg)
    do("gpsimd", lambda e: e.memset(onesk1b[:], 1.0), waits=[pg], inc=pg)
    do("gpsimd", lambda e: e.memset(onesk1f[:], 1.0), waits=[pg], inc=pg)
    do("gpsimd", lambda e: e.memset(fb[:], 0.0), waits=[pg], inc=pg)
    g_done = (pg, pg.v)
    ld = (s_ld, s_ld.v)

    def V(fn, waits=()):
        do("vector", fn, waits=list(waits) + [(pv, pv.v)], inc=pv)
        return (pv, pv.v)

    V(lambda e: e.tensor_tensor(out=m1[:, :], in0=lge(lg, 0), in1=lge(lg, 1), op=ALU.max), [ld])
    for ee in range(2, 8):
        V(lambda e, ee=ee: e.tensor_tensor(out=m1[:, :], in0=m1[:, :], in1=lge(lg, ee), op=ALU.max))
    first = True
    for ee in range(8):
        V(lambda e, ee=ee: e.tensor_tensor(out=tmp[:, :], in0=lge(lg, ee), in1=m1[:, :], op=ALU.is_equal))
        V(lambda e, ee=ee: e.scalar_tensor_tensor(out=tmp[:, :], in0=tmp[:, :], scalar=-BIG,
           in1=lge(lg, ee), op0=ALU.mult, op1=ALU.add))
        if first:
            V(lambda e: e.tensor_copy(out=m2[:, :], in_=tmp[:, :]))
            first = False
        else:
            V(lambda e: e.tensor_tensor(out=m2[:, :], in0=m2[:, :], in1=tmp[:, :], op=ALU.max))
    V(lambda e: e.tensor_sub(out=den[:, :], in0=m2[:, :], in1=m1[:, :]))
    do("scalar", lambda e: e.activation(out=den[:, :], in_=den[:, :], func=AF.Exp),
       waits=[(pv, pv.v)], inc=ps_)
    V(lambda e: e.tensor_scalar(out=den[:, :], in0=den[:, :], scalar1=1.0, scalar2=None, op0=ALU.add),
      [(ps_, ps_.v)])
    V(lambda e: e.reciprocal(out=den[:, :], in_=den[:, :]))
    for ee in range(8):
        V(lambda e, ee=ee: e.tensor_sub(out=tmp[:, :], in0=lge(lg, ee), in1=m1[:, :]))
        do("scalar", lambda e, ee=ee: e.activation(out=tmp[:, :], in_=tmp[:, :], func=AF.Exp),
           waits=[(pv, pv.v)], inc=ps_)
        V(lambda e, ee=ee: e.tensor_tensor(out=we[:, ee * NC:(ee + 1) * NC], in0=tmp[:, :], in1=den[:, :],
           op=ALU.mult), [(ps_, ps_.v)])
    for ee in range(8):
        V(lambda e, ee=ee: e.tensor_tensor(out=elig[:, :], in0=lge(lg, ee), in1=m2[:, :], op=ALU.is_ge),
          [g_done])
        V(lambda e, ee=ee: e.tensor_scalar(out=nf[:, :], in0=fb[:, :], scalar1=-1.0, scalar2=1.0,
           op0=ALU.mult, op1=ALU.add))
        V(lambda e, ee=ee: e.tensor_tensor(out=elig[:, :], in0=elig[:, :], in1=nf[:, :], op=ALU.mult))
        ecast = V(lambda e, ee=ee: e.tensor_copy(out=elig16[:, :], in_=elig[:, :]))
        for half in range(NC // 256):
            pass
        assert NC <= 512
        do("tensor", lambda e: e.matmul(out=P0[:, 0:NC], lhsT=triu16[:, :], rhs=elig16[:, :],
           start=True, stop=True), waits=[ecast, ld], inc=pt)
        do("tensor", lambda e: e.matmul(out=P1[0:1, 0:NC], lhsT=ones16[:, :], rhs=elig16[:, :],
           start=True, stop=True), waits=[ecast], inc=pt)
        cs_done = (pt, pt.v)
        V(lambda e: e.tensor_copy(out=rowA[:, :], in_=P1[0:1, 0:NC]), [cs_done])
        sh = 1
        cur_row, oth_row = rowA, rowB
        while sh < NC:
            def shift_add(e, cur_row=cur_row, oth_row=oth_row, sh=sh):
                return e.tensor_add(out=oth_row[:, sh:NC], in0=cur_row[:, sh:NC], in1=cur_row[:, 0:NC - sh])
            V(shift_add)
            V(lambda e, cur_row=cur_row, oth_row=oth_row, sh=sh: e.tensor_copy(
                out=oth_row[:, 0:sh], in_=cur_row[:, 0:sh]))
            cur_row, oth_row = oth_row, cur_row
            sh *= 2
        excl_done = V(lambda e, cur_row=cur_row: e.tensor_sub(out=cur_row[:, :], in0=cur_row[:, :],
           in1=P1[0:1, 0:NC]))
        do("tensor", lambda e, cur_row=cur_row: e.matmul(out=P1[:, 0:NC], lhsT=onesk1f[:, :],
           rhs=cur_row[:, :], start=True, stop=True), waits=[excl_done], inc=pt)
        bc_done = (pt, pt.v)
        V(lambda e: e.tensor_copy(out=tmp[:, :], in_=P1[0:1, 0:NC]) if False else
          e.tensor_copy(out=tmp[:, :], in_=P1[:, 0:NC]), [bc_done])
        V(lambda e: e.tensor_tensor(out=kept[:, :], in0=P0[:, 0:NC], in1=tmp[:, :], op=ALU.add))
        V(lambda e: e.tensor_scalar(out=kept[:, :], in0=kept[:, :], scalar1=float(cap), scalar2=None,
           op0=ALU.is_le))
        V(lambda e: e.tensor_tensor(out=kept[:, :], in0=kept[:, :], in1=elig[:, :], op=ALU.mult))
        V(lambda e, ee=ee: e.tensor_tensor(out=ge[:, ee * NC:(ee + 1) * NC], in0=kept[:, :],
           in1=we[:, ee * NC:(ee + 1) * NC], op=ALU.mult))
        V(lambda e: e.tensor_sub(out=elig[:, :], in0=elig[:, :], in1=kept[:, :]))
        V(lambda e: e.tensor_add(out=fb[:, :], in0=fb[:, :], in1=elig[:, :]))
    V(lambda e: e.tensor_scalar(out=nf[:, :], in0=fb[:, :], scalar1=-1.0, scalar2=1.0,
       op0=ALU.mult, op1=ALU.add))
    for ee in range(8):
        m = V(lambda e, ee=ee: e.tensor_tensor(out=ge[:, ee * NC:(ee + 1) * NC],
           in0=ge[:, ee * NC:(ee + 1) * NC], in1=nf[:, :], op=ALU.mult))
        dma("gpsimd", lambda e, ee=ee: e.dma_start(out=ge_ap[ee, :, :], in_=ge[:, ee * NC:(ee + 1) * NC]),
            s_st, waits=[m])
    mfb = V(lambda e: e.tensor_copy(out=tmp[:, :], in_=fb[:, :]))
    dma("gpsimd", lambda e: e.dma_start(out=fb_ap[:, :], in_=tmp[:, :]), s_st, waits=[mfb])
    do("gpsimd", None, waits=[(s_st, s_st.v)])


def ph_moe_apply(k: KB, T, xnfm16_ap, w1s_ap, b1s_ap, w2s_ap, b2s_ap,
                 fw1_ap, fb1_ap, fw2_ap, fb2_ap, gates_ap, yacc_ap, uniq="me"):
    nc, do, dma = k.nc, k.do, k.dmago
    D, DFF = 1024, 4096
    TT = 512
    NT = T // TT
    pv, ps_, pt, pg = k.pg["vector"], k.pg["scalar"], k.pg["tensor"], k.pg["gpsimd"]
    U = uniq

    w116 = [k.sb(f"{U}_w116{p}", [128, 8 * DFF], BF16) for p in range(1)]
    w216 = [k.sb(f"{U}_w216{p}", [128, 32 * D], BF16) for p in range(1)]
    bcol1 = k.sb(f"{U}_bc1", [128, 32], F32)
    b2row = k.sb(f"{U}_b2row", [1, D], F32)
    b2b = k.sb(f"{U}_b2b", [128, D], F32)
    xn16 = [k.sb(f"{U}_xn16{p}", [128, 8 * TT], BF16) for p in range(2)]
    y116_t = k.sb(f"{U}_y116", [128, 32 * TT], BF16)
    y116 = [y116_t, y116_t]
    yst = k.sb(f"{U}_yst", [128, D], F32)
    gcols = k.sb(f"{U}_gcols", [128, T // 128], F32)
    onesk1f = k.sb(f"{U}_onesk1f", [1, 128], F32)
    F = [k.ps(f"{U}_F{i}", [128, 512], F32) for i in range(6)]

    s_w = k.sem(f"{U}_sw")
    s_x = [k.sem(f"{U}_sx{p}") for p in range(2)]
    s_g = k.sem(f"{U}_sg")
    s_st = k.sem(f"{U}_sst")

    do("gpsimd", lambda e: e.memset(onesk1f[:], 1.0), inc=pg)
    ones_done = (pg, pg.v)

    NPASS = 9
    pass_store_done = None
    wl_prev = None
    w116_free = None
    w216_free = None
    w1ev_prev = None
    y116_rd = {}
    bias_free = None
    for pi in range(NPASS):
        if pi == 0:
            w1src, b1src, w2src, b2src = fw1_ap, fb1_ap, fw2_ap, fb2_ap
        else:
            ee = pi - 1
            w1src, b1src, w2src, b2src = (w1s_ap[ee], b1s_ap[ee], w2s_ap[ee], b2s_ap[ee])
        wl = []
        for c in range(8):
            wl.append(dma("gpsimd", lambda e, c=c, w1src=w1src: e.dma_start(
                out=w116[0][:, c * DFF:(c + 1) * DFF], in_=w1src[c * 128:(c + 1) * 128, :]),
                s_w, waits=[wl_prev] if wl_prev else [], tag=f"w1_{pi}_{c}"))
        for c in range(32):
            wl.append(dma("gpsimd", lambda e, c=c, w2src=w2src: e.dma_start(
                out=w216[0][:, c * D:(c + 1) * D], in_=w2src[c * 128:(c + 1) * 128, :]),
                s_w, waits=[], tag=f"w2_{pi}_{c}"))
        for ci, c in enumerate(range(32)):
            dma("sync", lambda e, c=c, b1src=b1src: e.dma_start(out=bcol1[:, c:c + 1],
                in_=b1src[c * 128:(c + 1) * 128]), s_g,
                waits=[bias_free] if (ci == 0 and bias_free) else [])
        dma("sync", lambda e, b2src=b2src: e.dma_start(out=b2row[:, :],
            in_=b2src[:].rearrange("(o d) -> o d", o=1)), s_g)
        dma("sync", lambda e, pi=pi: e.dma_start(out=gcols[:, :], in_=gates_ap[pi, :, :]), s_g)
        small_ld = (s_g, s_g.v)
        w_done = wl[-1]
        bb = []
        for half in range(2):
            do("tensor", lambda e, half=half: e.matmul(out=F[4][:, :], lhsT=onesk1f[:, :],
               rhs=b2row[:, half * 512:(half + 1) * 512], start=True, stop=True),
               waits=[small_ld, ones_done] + ([bb[-1]] if bb else []) + ([pass_store_done] if pass_store_done else []), inc=pt)
            do("vector", lambda e, half=half: e.tensor_copy(out=b2b[:, half * 512:(half + 1) * 512],
               in_=F[4][:, :]), waits=[(pt, pt.v)], inc=pv)
            bb.append((pv, pv.v))
        bcast_done = bb[-1]

        tile_done = {}
        for tt in range(NT):
            p = tt % 2
            tw = [tile_done[tt - 2]] if tt >= 2 else []
            if tt < 2 and wl_prev is not None and pi > 0:
                tw = tw + [wl_prev]
            lx = None
            for c in range(8):
                lx = dma("sync", lambda e, tt=tt, c=c, p=p: e.dma_start(
                    out=xn16[p][:, c * TT:(c + 1) * TT],
                    in_=xnfm16_ap[c * 128:(c + 1) * 128, tt * TT:(tt + 1) * TT]), s_x[p],
                    waits=tw if c == 0 else [])
            ev = {}
            for ff in range(32):
                fbk = ff % 2
                mw = [lx, w_done, small_ld]
                if ff >= 2:
                    mw.append(ev[ff - 2])
                elif tt >= 1:
                    mw.append(tile_done[tt - 1])
                def mm1(e, ff=ff, fbk=fbk, p=p):
                    for c in range(8):
                        mm = e.matmul(out=F[fbk][:, :], lhsT=w116[0][:, c * DFF + ff * 128: c * DFF + (ff + 1) * 128],
                                      rhs=xn16[p][:, c * TT:(c + 1) * TT], start=(c == 0), stop=(c == 7))
                    return mm
                do("tensor", mm1, waits=mw, inc=pt)
                do("scalar", lambda e, ff=ff, fbk=fbk: e.activation(out=y116[:, ff * TT:(ff + 1) * TT],
                   in_=F[fbk][:, :], func=AF.Gelu, bias=bcol1[:, ff:ff + 1]), waits=[(pt, pt.v)], inc=ps_)
                ev[ff] = (ps_, ps_.v)
            y1_done = (ps_, ps_.v)
            for sub in range(TT // 128):
                for half in range(2):
                    fbk = 2 + (sub * 2 + half) % 2
                    mw = [y1_done]
                    prev_it = sub * 2 + half - 2
                    if prev_it >= 0:
                        mw.append(tile_done.get((tt, prev_it), (pv, pv.v)))
                    def mm2(e, sub=sub, half=half, fbk=fbk):
                        for c in range(32):
                            mm = e.matmul(out=F[fbk][:, :],
                                          lhsT=y116[:, c * TT + sub * 128: c * TT + (sub + 1) * 128],
                                          rhs=w216[0][:, c * D + half * 512: c * D + (half + 1) * 512],
                                          start=(c == 0), stop=(c == 31))
                        return mm
                    do("tensor", mm2, waits=mw, inc=pt)
                    do("vector", lambda e, half=half, fbk=fbk: e.scalar_tensor_tensor(
                       out=yst[:, half * 512:(half + 1) * 512], in0=F[fbk][:, :], scalar=1.0,
                       in1=b2b[:, half * 512:(half + 1) * 512], op0=ALU.mult, op1=ALU.add),
                       waits=[(pt, pt.v), bcast_done, (pv, pv.v), (s_st, s_st.v)], inc=pv)
                    gi = tt * (TT // 128) + sub
                    do("vector", lambda e, half=half, gi=gi: e.tensor_scalar(
                       out=yst[:, half * 512:(half + 1) * 512], in0=yst[:, half * 512:(half + 1) * 512],
                       scalar1=gcols[:, gi:gi + 1], scalar2=None, op0=ALU.mult),
                       waits=[(pv, pv.v)], inc=pv)
                    tile_done[(tt, sub * 2 + half)] = (pv, pv.v)
                gi = tt * (TT // 128) + sub
                stw = [tile_done[(tt, sub * 2)], tile_done[(tt, sub * 2 + 1)]]
                if pass_store_done is not None:
                    stw.append(pass_store_done)
                dma("gpsimd", lambda e, tt=tt, sub=sub, pi=pi: e.dma_start(
                    out=yacc_ap[tt * TT + sub * 128: tt * TT + (sub + 1) * 128, :], in_=yst[:, :],
                    accum_op=(ALU.add if pi > 0 else ALU.bypass)), s_st, waits=stw,
                    tag=f"acc_{pi}_{tt}_{sub}")
            tile_done[tt] = (pv, pv.v)
        pass_store_done = (s_st, s_st.v)
        wl_prev = (pv, pv.v)
    do("gpsimd", None, waits=[(s_st, s_st.v)])


def ph_b0_combine(k: KB, T, x_ap, sfm_ap, yacc_ap, x1_ap, xn1fm16_ap, xn116_ap, uniq="bc"):
    nc, do, dma = k.nc, k.do, k.dmago
    D = 1024
    nt = T // 128
    pv, ps_, pt, pg = k.pg["vector"], k.pg["scalar"], k.pg["tensor"], k.pg["gpsimd"]
    U = uniq
    xt = [k.sb(f"{U}_xt{p}", [128, D], F32) for p in range(2)]
    yt = [k.sb(f"{U}_yt{p}", [128, D], F32) for p in range(2)]
    sfmt = [k.sb(f"{U}_sf{p}", [128, D], F32) for p in range(2)]
    stok = [k.sb(f"{U}_st{p}", [128, D], F32) for p in range(2)]
    gt = k.sb(f"{U}_gt", [128, D], F32)
    x1t = [k.sb(f"{U}_x1{p}", [128, D], F32) for p in range(2)]
    sqs = k.sb(f"{U}_sqs", [128, D], F32)
    stt_ = [k.sb(f"{U}_stt{p}", [128, 8], F32) for p in range(2)]
    xnt = [k.sb(f"{U}_xnt{p}", [128, D], F32) for p in range(2)]
    xn16t = k.sb(f"{U}_xn16t", [128, D], BF16)
    idf = k.sb(f"{U}_idf", [128, 128], F32)
    idb = k.sb(f"{U}_idb", [128, 128], BF16)
    psT = [k.ps(f"{U}_psT{p}", [128, 128], F32) for p in range(2)]
    psB = [k.ps(f"{U}_psB{p}", [128, 128], BF16) for p in range(2)]
    c_idf = k.const(f"{U}_idfc", np.eye(128, dtype=np.float32))
    c_idb = k.const(f"{U}_idbc", np.eye(128, dtype=ml_dtypes.bfloat16))
    s_ld = [k.sem(f"{U}_sld{p}") for p in range(2)]
    s_st = [k.sem(f"{U}_sst{p}") for p in range(2)]
    s_c = k.sem(f"{U}_sc")

    dma("sync", lambda e: e.dma_start(out=idf[:, :], in_=c_idf[:, :]), s_c)
    dma("sync", lambda e: e.dma_start(out=idb[:, :], in_=c_idb[:, :]), s_c)
    cdone = (s_c, s_c.v)

    tile_done = {}
    st_done = {}
    for i in range(nt):
        p = i % 2
        tw = [tile_done[i - 2]] if i >= 2 else []
        dma("sync", lambda e, i=i, p=p: e.dma_start(out=xt[p][:, :], in_=x_ap[i * 128:(i + 1) * 128, :]),
            s_ld[p], waits=tw)
        dma("sync", lambda e, i=i, p=p: e.dma_start(out=yt[p][:, :], in_=yacc_ap[i * 128:(i + 1) * 128, :]),
            s_ld[p])
        for c in range(8):
            dma("sync", lambda e, i=i, p=p, c=c: e.dma_start(out=sfmt[p][:, c * 128:(c + 1) * 128],
                in_=sfm_ap[c * 128:(c + 1) * 128, i * 128:(i + 1) * 128]), s_ld[p])
        ld = (s_ld[p], s_ld[p].v)
        for c in range(8):
            cp = c % 2
            tpw = [ld, cdone]
            if c >= 2 or i >= 1:
                tpw.append((ps_, ps_.v))
            do("tensor", lambda e, p=p, c=c, cp=cp: e.transpose(out=psT[cp][:, :],
               in_=sfmt[p][:, c * 128:(c + 1) * 128], identity=idf[:, :]), waits=tpw, inc=pt)
            do("scalar", lambda e, p=p, c=c, cp=cp: e.copy(out=stok[p][:, c * 128:(c + 1) * 128],
               in_=psT[cp][:, :]), waits=[(pt, pt.v)], inc=ps_)
        s_done = (ps_, ps_.v)
        do("scalar", lambda e, p=p: e.activation(out=gt[:, :], in_=yt[p][:, :], func=AF.Silu),
           waits=[ld, (ps_, ps_.v)], inc=ps_)
        gdone = (ps_, ps_.v)
        do("vector", lambda e, p=p: e.tensor_tensor(out=gt[:, :], in0=gt[:, :], in1=stok[p][:, :],
           op=ALU.mult), waits=[gdone, s_done, (pv, pv.v)], inc=pv)
        do("vector", lambda e, p=p: e.tensor_add(out=x1t[p][:, :], in0=gt[:, :], in1=xt[p][:, :]),
           waits=[(pv, pv.v)] + ([st_done[i - 2]] if i >= 2 else []), inc=pv)
        x1_done = (pv, pv.v)
        dma("gpsimd", lambda e, i=i, p=p: e.dma_start(out=x1_ap[i * 128:(i + 1) * 128, :],
            in_=x1t[p][:, :]), s_st[p], waits=[x1_done])
        do("vector", lambda e, p=p: e.reduce_sum(out=stt_[p][:, 0:1], in_=x1t[p][:, :], axis=AX.X),
           waits=[x1_done], inc=pv)
        do("scalar", lambda e, p=p: e.activation(out=sqs[:, :], in_=x1t[p][:, :], func=AF.Square,
           accum_out=stt_[p][:, 1:2]), waits=[x1_done, (ps_, ps_.v)], inc=ps_)
        sq_done = (ps_, ps_.v)
        do("vector", lambda e, p=p: e.tensor_scalar(out=stt_[p][:, 2:3], in0=stt_[p][:, 0:1],
           scalar1=1.0 / D, scalar2=None, op0=ALU.mult), waits=[(pv, pv.v), sq_done], inc=pv)
        do("vector", lambda e, p=p: e.tensor_scalar(out=stt_[p][:, 3:4], in0=stt_[p][:, 1:2],
           scalar1=1.0 / D, scalar2=None, op0=ALU.mult), waits=[(pv, pv.v)], inc=pv)
        do("vector", lambda e, p=p: e.scalar_tensor_tensor(out=stt_[p][:, 4:5], in0=stt_[p][:, 2:3],
           scalar=1.0, in1=stt_[p][:, 2:3], op0=ALU.mult, op1=ALU.mult), waits=[(pv, pv.v)], inc=pv)
        do("vector", lambda e, p=p: e.tensor_sub(out=stt_[p][:, 3:4], in0=stt_[p][:, 3:4],
           in1=stt_[p][:, 4:5]), waits=[(pv, pv.v)], inc=pv)
        do("vector", lambda e, p=p: e.tensor_scalar(out=stt_[p][:, 3:4], in0=stt_[p][:, 3:4],
           scalar1=1e-5, scalar2=None, op0=ALU.add), waits=[(pv, pv.v)], inc=pv)
        do("scalar", lambda e, p=p: e.activation(out=stt_[p][:, 5:6], in_=stt_[p][:, 3:4], func=AF.Sqrt),
           waits=[(pv, pv.v)], inc=ps_)
        do("vector", lambda e, p=p: e.reciprocal(out=stt_[p][:, 6:7], in_=stt_[p][:, 5:6]),
           waits=[(ps_, ps_.v)], inc=pv)
        do("vector", lambda e, p=p: e.tensor_scalar(out=xnt[p][:, :], in0=x1t[p][:, :],
           scalar1=stt_[p][:, 2:3], scalar2=stt_[p][:, 6:7], op0=ALU.subtract, op1=ALU.mult),
           waits=[(pv, pv.v)] + ([tile_done[i - 2]] if i >= 2 else []), inc=pv)
        xn_done = (pv, pv.v)
        if i == 0:
            dma("gpsimd", lambda e, p=p: e.dma_start(out=xn116_ap[:, :], in_=xnt[p][0:16, :]),
                s_st[p], waits=[xn_done])
        do("vector", lambda e, p=p: e.tensor_copy(out=xn16t[:, :], in_=xnt[p][:, :]),
           waits=[xn_done, (pv, pv.v)] + ([st_done[i - 1]] if i >= 1 else []), inc=pv)
        cast_done = (pv, pv.v)
        for c in range(8):
            cp = c % 2
            tpw = [cast_done, cdone]
            tpw.append((pv, pv.v) if c < 2 else (pv, pv.v))
            if c >= 2 or i >= 1:
                tpw.append((pv, pv.v))
            do("tensor", lambda e, c=c, cp=cp: e.transpose(out=psB[cp][:, :],
               in_=xn16t[:, c * 128:(c + 1) * 128], identity=idb[:, :]),
               waits=tpw + [(pv, pv.v)], inc=pt)
            tp = (pt, pt.v)
            do("vector", lambda e, c=c, cp=cp: e.tensor_copy(out=xn16t[:, c * 128:(c + 1) * 128],
               in_=psB[cp][:, :]) if False else e.tensor_copy(out=xn16t[:, c * 128:(c + 1) * 128],
               in_=psB[cp][:, :]), waits=[tp], inc=pv)
            dma("gpsimd", lambda e, i=i, c=c: e.dma_start(
                out=xn1fm16_ap[c * 128:(c + 1) * 128, i * 128:(i + 1) * 128],
                in_=xn16t[:, c * 128:(c + 1) * 128]), s_st[p], waits=[(pv, pv.v)])
        tile_done[i] = (pv, pv.v)
        st_done[i] = (s_st[p], s_st[p].v)
    do("gpsimd", None, waits=[(s_st[0], s_st[0].v), (s_st[1], s_st[1].v)])


def ph_l1_ffn(k: KB, T, xnfm_ap, wr_ap, wk_ap, wv_ap, mur_ap, muk_ap, g1_ap, uniq="lf"):
    nc, do, dma = k.nc, k.do, k.dmago
    D = 1024
    TT = 512
    NT = T // TT
    pv, ps_, pt, pg = k.pg["vector"], k.pg["scalar"], k.pg["tensor"], k.pg["gpsimd"]
    U = uniq
    wtmp = k.sb(f"{U}_wtmp", [128, 1024], F32)
    w16s = {n: k.sb(f"{U}_w16{n}", [128, 8192], BF16) for n in "rkv"}
    wT16 = {n: k.sb(f"{U}_wT{n}", [128, 8192], BF16) for n in "rkv"}
    mcol = {n: k.sb(f"{U}_mc{n}", [128, 8], F32) for n in ("r", "kk")}
    xc = [k.sb(f"{U}_xc{p}", [128, 8 * TT], BF16) for p in range(2)]
    xpcol = [k.sb(f"{U}_xp{p}", [128, 8], BF16) for p in range(2)]
    xr16 = k.sb(f"{U}_xr16", [128, 8 * TT], BF16)
    xk16 = k.sb(f"{U}_xk16", [128, 8 * TT], BF16)
    ka16 = k.sb(f"{U}_ka16", [128, 8 * TT], BF16)
    kr16 = k.sb(f"{U}_kr16", [128, TT], F32)
    rt16 = k.sb(f"{U}_rt16", [128, 1024], BF16)
    vt = k.sb(f"{U}_vt", [128, 1024], F32)
    gst = k.sb(f"{U}_gst", [128, 1024], BF16)
    idb = k.sb(f"{U}_idb", [128, 128], BF16)
    F = [k.ps(f"{U}_F{i}", [128, 512], F32) for i in range(6)]
    PB = k.ps(f"{U}_PB", [128, 1024], BF16)
    c_idb = k.const(f"{U}_idbc", np.eye(128, dtype=ml_dtypes.bfloat16))
    s_w = k.sem(f"{U}_sw")
    s_b = k.sem(f"{U}_sb")
    s_x = [k.sem(f"{U}_sx{p}") for p in range(2)]
    s_st = [k.sem(f"{U}_sstq{p}") for p in range(2)]

    dma("sync", lambda e: e.dma_start(out=idb[:, :], in_=c_idb[:, :]), s_b)
    for (n2, src2) in (("r", mur_ap), ("kk", muk_ap)):
        for c in range(8):
            dma("sync", lambda e, n2=n2, src2=src2, c=c: e.dma_start(out=mcol[n2][:, c:c + 1],
                in_=src2[c * 128:(c + 1) * 128]), s_b)
    wsrc = {"r": wr_ap, "k": wk_ap, "v": wv_ap}
    wload = None
    for n in "rkv":
        for c in range(8):
            lw = dma("sync", lambda e, n=n, c=c: e.dma_start(out=wtmp[:, :],
                     in_=wsrc[n][c * 128:(c + 1) * 128, :]), s_w, waits=[wload] if wload else [])
            do("vector", lambda e, n=n, c=c: e.tensor_copy(out=w16s[n][:, c * 1024:(c + 1) * 1024],
               in_=wtmp[:, :]), waits=[lw], inc=pv)
            wload = (pv, pv.v)
    tpev = None
    for n in "rkv":
        for ci in range(8):
            for dc2 in range(0, 8, 4):
                def mtw(e, n=n, ci=ci, dc2=dc2):
                    for j in range(4):
                        dc = dc2 + j
                        mm = e.transpose(out=PB[:, j * 128:(j + 1) * 128],
                                         in_=w16s[n][:, dc * 1024 + ci * 128: dc * 1024 + (ci + 1) * 128],
                                         identity=idb[:, :])
                    return mm
                tw = [wload, (s_b, s_b.v)]
                if tpev:
                    tw.append(tpev)
                do("tensor", mtw, waits=tw, inc=pt)
                do("scalar", lambda e, n=n, ci=ci, dc2=dc2: e.copy(
                   out=wT16[n][:, ci * 1024 + dc2 * 128: ci * 1024 + (dc2 + 4) * 128],
                   in_=PB[:, 0:512]), waits=[(pt, pt.v)], inc=ps_)
                tpev = (ps_, ps_.v)
    wT_done = tpev
    b_done = (s_b, s_b.v)

    tile_done = {}
    st_done = {}
    for tt in range(NT):
        p = tt % 2
        tw = [tile_done[tt - 2]] if tt >= 2 else []
        lx = None
        for c in range(8):
            lx = dma("sync", lambda e, tt=tt, c=c, p=p: e.dma_start(
                out=xc[p][:, c * TT:(c + 1) * TT],
                in_=xnfm_ap[c * 128:(c + 1) * 128, tt * TT:(tt + 1) * TT]), s_x[p],
                waits=tw if c == 0 else [])
        if tt > 0:
            for c in range(8):
                lx = dma("sync", lambda e, tt=tt, c=c, p=p: e.dma_start(
                    out=xpcol[p][:, c:c + 1],
                    in_=xnfm_ap[c * 128:(c + 1) * 128, tt * TT - 1:tt * TT],
                    allow_slow_non_contiguous=True), s_x[p])
        else:
            do("gpsimd", lambda e, p=p: e.memset(xpcol[p][:, :], 0.0), inc=pg)
        ldx = lx
        gz = (pg, pg.v)
        mixw = [ldx, b_done, gz] + ([tile_done[tt - 1]] if tt >= 1 else [])
        for (dst, mn) in ((xr16, "r"), (xk16, "kk")):
            for c in range(8):
                do("vector", lambda e, dst=dst, mn=mn, c=c, p=p: e.tensor_scalar(
                   out=dst[:, c * TT:(c + 1) * TT], in0=xc[p][:, c * TT:(c + 1) * TT],
                   scalar1=mcol[mn][:, c:c + 1], scalar2=None, op0=ALU.mult),
                   waits=mixw + [(pv, pv.v)], inc=pv)
                def addprev(e, dst=dst, mn=mn, c=c, p=p):
                    return e.scalar_tensor_tensor(out=dst[:, c * TT + 1:(c + 1) * TT],
                        in0=xc[p][:, c * TT:(c + 1) * TT - 1], scalar=1.0,
                        in1=dst[:, c * TT + 1:(c + 1) * TT], op0=ALU.mult, op1=ALU.bypass) if False else None
                do("vector", lambda e, dst=dst, mn=mn, c=c, p=p: e.scalar_tensor_tensor(
                   out=dst[:, c * TT + 1:(c + 1) * TT], in0=xc[p][:, c * TT:(c + 1) * TT - 1],
                   scalar=-1.0, in1=dst[:, c * TT + 1:(c + 1) * TT], op0=ALU.bypass, op1=ALU.bypass)
                   if False else e.tensor_scalar(out=kr16[:, 0:TT - 1], in0=xc[p][:, c * TT:(c + 1) * TT - 1],
                   scalar1=mcol[mn][:, c:c + 1], scalar2=None, op0=ALU.mult),
                   waits=[(pv, pv.v)], inc=pv)
                do("vector", lambda e, dst=dst, c=c, p=p: e.tensor_sub(out=kr16[:, 0:TT - 1],
                   in0=xc[p][:, c * TT:(c + 1) * TT - 1], in1=kr16[:, 0:TT - 1]),
                   waits=[(pv, pv.v)], inc=pv)
                do("vector", lambda e, dst=dst, c=c: e.tensor_add(out=dst[:, c * TT + 1:(c + 1) * TT],
                   in0=dst[:, c * TT + 1:(c + 1) * TT], in1=kr16[:, 0:TT - 1]),
                   waits=[(pv, pv.v)], inc=pv)
                do("vector", lambda e, mn=mn, c=c, p=p: e.tensor_scalar(out=kr16[:, TT - 1:TT],
                   in0=xpcol[p][:, c:c + 1], scalar1=mcol[mn][:, c:c + 1], scalar2=None, op0=ALU.mult),
                   waits=[(pv, pv.v)], inc=pv)
                do("vector", lambda e, c=c, p=p: e.tensor_sub(out=kr16[:, TT - 1:TT],
                   in0=xpcol[p][:, c:c + 1], in1=kr16[:, TT - 1:TT]), waits=[(pv, pv.v)], inc=pv)
                do("vector", lambda e, dst=dst, c=c: e.tensor_add(out=dst[:, c * TT:c * TT + 1],
                   in0=dst[:, c * TT:c * TT + 1], in1=kr16[:, TT - 1:TT]), waits=[(pv, pv.v)], inc=pv)
        mix_done = (pv, pv.v)
        kev = {}
        for f in range(8):
            fb = f % 2
            mw = [mix_done, wT_done]
            if f >= 2:
                mw.append(kev[f - 2])
            def mmk(e, f=f, fb=fb):
                for c in range(8):
                    mm = e.matmul(out=F[fb][:, :], lhsT=wT16["k"][:, c * 1024 + f * 128: c * 1024 + (f + 1) * 128],
                                  rhs=xk16[:, c * TT:(c + 1) * TT], start=(c == 0), stop=(c == 7))
                return mm
            do("tensor", mmk, waits=mw, inc=pt)
            do("scalar", lambda e, f=f, fb=fb: e.activation(out=kr16[:, :], in_=F[fb][:, :], func=AF.Relu),
               waits=[(pt, pt.v), (ps_, ps_.v), (pv, pv.v)], inc=ps_)
            do("vector", lambda e, f=f: e.tensor_tensor(out=ka16[:, f * TT:(f + 1) * TT], in0=kr16[:, :],
               in1=kr16[:, :], op=ALU.mult), waits=[(ps_, ps_.v), (pv, pv.v)], inc=pv)
            kev[f] = (pv, pv.v)
        ka_done = (pv, pv.v)
        for sub in range(TT // 128):
            for half in range(2):
                fb = 2 + half
                mw = [mix_done, wT_done]
                if tt + sub + half > 0:
                    mw.append((ps_, ps_.v))
                def mmr(e, sub=sub, half=half, fb=fb):
                    for c in range(8):
                        mm = e.matmul(out=F[fb][:, :],
                                      lhsT=xr16[:, c * TT + sub * 128: c * TT + (sub + 1) * 128],
                                      rhs=wT16["r"][:, c * 1024 + half * 512: c * 1024 + (half + 1) * 512],
                                      start=(c == 0), stop=(c == 7))
                    return mm
                do("tensor", mmr, waits=mw, inc=pt)
                do("scalar", lambda e, half=half, fb=fb: e.activation(
                   out=rt16[:, half * 512:(half + 1) * 512],
                   in_=F[fb][:, :], func=AF.Sigmoid), waits=[(pt, pt.v), (pv, pv.v)], inc=ps_)
            r_done = (ps_, ps_.v)
            for half in range(2):
                fb = 4 + half
                mw = [ka_done]
                if tt + sub + half > 0:
                    mw.append((pv, pv.v))
                def mmv(e, sub=sub, half=half, fb=fb):
                    for c in range(8):
                        mm = e.matmul(out=F[fb][:, :],
                                      lhsT=ka16[:, c * TT + sub * 128: c * TT + (sub + 1) * 128],
                                      rhs=wT16["v"][:, c * 1024 + half * 512: c * 1024 + (half + 1) * 512],
                                      start=(c == 0), stop=(c == 7))
                    return mm
                do("tensor", mmv, waits=mw, inc=pt)
                do("vector", lambda e, half=half, fb=fb: e.tensor_copy(out=vt[:, half * 512:(half + 1) * 512],
                   in_=F[fb][:, :]), waits=[(pt, pt.v), (pv, pv.v), (ps_, ps_.v)], inc=pv)
            v_done = (pv, pv.v)
            do("vector", lambda e: e.tensor_tensor(out=vt[:, :], in0=vt[:, :],
               in1=rt16[:, :], op=ALU.mult), waits=[v_done, r_done], inc=pv)
            do("scalar", lambda e: e.activation(out=gst[:, :], in_=vt[:, :], func=AF.Silu),
               waits=[(pv, pv.v), (s_st[p], s_st[p].v)], inc=ps_)
            g_done = (ps_, ps_.v)
            dma("gpsimd", lambda e, tt=tt, sub=sub: e.dma_start(
                out=g1_ap[tt * TT + sub * 128: tt * TT + (sub + 1) * 128, :], in_=gst[:, :]),
                s_st[p], waits=[g_done])
        tile_done[tt] = (pv, pv.v)
        st_done[tt] = (s_st[p], s_st[p].v)
    do("gpsimd", None, waits=[(s_st[0], s_st[0].v), (s_st[1], s_st[1].v)])


def ph_l1_small(k: KB, xall_ap, qw_ap, qb_ap, kw_ap, vw_ap, vb_ap, ow_ap, ob_ap,
                lna_ap, lnb_ap, lnc_ap, lnd_ap, lqw_ap, lqb_ap, lkw_ap, lkb_ap,
                lvw_ap, lvb_ap, s1_ap, uniq="ls"):
    nc, do, dma = k.nc, k.do, k.dmago
    D, H, HD, B = 1024, 16, 64, 8
    NB = H * B
    pv, ps_, pt, pg = k.pg["vector"], k.pg["scalar"], k.pg["tensor"], k.pg["gpsimd"]
    U = uniq
    MAX_IT = 3
    THRESH, FACTOR = 0.01, 0.1
    DIFF_SCALE = 1.0 / (8 * 16 * 4096 * 64)

    xall = k.sb(f"{U}_xall", [128, D], F32)
    lnxT = k.sb(f"{U}_lnxT", [128, D], BF16)
    wtmp = k.sb(f"{U}_wtmp", [128, D], F32)
    lnx = xall
    wT = {n: k.sb(f"{U}_wT{n}", [128, 8192], BF16) for n in "qkv"}
    wT["o"] = wT["q"]
    lncol = {n: k.sb(f"{U}_lncol{n}", [128, 8], F32) for n in ("a", "b")}
    stt_ = k.sb(f"{U}_stt", [128, 8], F32)
    sqs = k.sb(f"{U}_sqs", [128, D], F32)
    bb = {n: k.sb(f"{U}_bb{n}", [128, D], F32) for n in ("q", "v")}
    bb["o"] = bb["q"]
    brow = k.sb(f"{U}_brow", [1, D], F32)
    ptok = k.sb(f"{U}_ptok", [128, D], F32)
    ptok16 = k.sb(f"{U}_ptok16", [128, D], BF16)
    qH = k.sb(f"{U}_qH", [64, H * 128], BF16)
    kH = k.sb(f"{U}_kH", [64, H * 128], BF16)
    vH = k.sb(f"{U}_vH", [64, H * 128], BF16)
    lsw = k.sb(f"{U}_lsw", [64, 64 * 3], BF16)
    lsb = k.sb(f"{U}_lsb", [64, 8], F32)
    lvbrow8 = k.sb(f"{U}_lvbrow8", [1, 512], F32)
    lvbb = k.sb(f"{U}_lvbb", [16, 512], F32)
    knr = k.sb(f"{U}_knr", [64, H * 128], F32)
    kn16 = k.sb(f"{U}_kn16", [64, H * 128], BF16)
    qn16 = k.sb(f"{U}_qn16", [64, H * 128], BF16)
    viTT = k.sb(f"{U}_viTT", [16, NB * 64], BF16)
    lgR = k.sb(f"{U}_lgR", [16, NB * 16], F32)
    mxr = k.sb(f"{U}_mxr", [16, NB], F32)
    smr = k.sb(f"{U}_smr", [16, NB], F32)
    attnR = k.sb(f"{U}_attnR", [16, NB * 16], BF16)
    attnT16 = k.sb(f"{U}_attnT16", [16, NB * 16], BF16)
    iterR = k.sb(f"{U}_iterR", [16, NB * 64], BF16)
    aoR = k.sb(f"{U}_aoR", [16, NB * 64], BF16)
    qcurH = k.sb(f"{U}_qcurH", [64, H * 128], F32)
    qitH = knr2 = None
    dsc = k.sb(f"{U}_dsc", [16, 20], F32)
    sc1 = k.sb(f"{U}_sc1", [1, 8], F32)
    g16 = k.sb(f"{U}_g16", [16, 2], F32)
    g64 = k.sb(f"{U}_g64", [64, 2], F32)
    aF16 = k.sb(f"{U}_aF16", [128, 1024], BF16)
    s1t = ptok
    ln64r = k.sb(f"{U}_ln64r", [1, 2048], F32)
    idb = k.sb(f"{U}_idb", [128, 128], BF16)
    ones64f = k.sb(f"{U}_ones64f", [64, 1], F32)
    ones16f = k.sb(f"{U}_ones16f", [16, 1], F32)
    onesk1f = k.sb(f"{U}_onesk1f", [1, 128], F32)
    F = [k.ps(f"{U}_F{i}", [128, 512], F32) for i in range(6)]
    PB = [k.ps(f"{U}_PB{i}", [128, 1024], BF16) for i in range(2)]
    c_idb = k.const(f"{U}_idbc", np.eye(128, dtype=ml_dtypes.bfloat16))
    s_ld = k.sem(f"{U}_sld")
    s_w = k.sem(f"{U}_sw")
    s_st = k.sem(f"{U}_sstq")

    LNEPS = 1e-5

    dma("sync", lambda e: e.dma_start(out=xall[:, :], in_=xall_ap[:, :]), s_ld)
    dma("sync", lambda e: e.dma_start(out=idb[:, :], in_=c_idb[:, :]), s_ld)
    for (n, src) in (("a", lna_ap), ("b", lnb_ap)):
        for c in range(8):
            dma("sync", lambda e, n=n, src=src, c=c: e.dma_start(out=lncol[n][:, c:c + 1],
                in_=src[c * 128:(c + 1) * 128]), s_ld)
    dma("sync", lambda e: e.dma_start(out=lsb[:, 0:1], in_=lqb_ap[:]), s_ld)
    dma("sync", lambda e: e.dma_start(out=lsb[:, 1:2], in_=lkb_ap[:]), s_ld)
    dma("sync", lambda e: e.dma_start(out=lsb[:, 2:3], in_=lnc_ap[:]), s_ld)
    dma("sync", lambda e: e.dma_start(out=lsb[:, 3:4], in_=lnd_ap[:]), s_ld)
    for j in range(8):
        dma("sync", lambda e, j=j: e.dma_start(out=lvbrow8[:, j * 64:(j + 1) * 64],
            in_=lvb_ap[:].rearrange("(o d) -> o d", o=1)), s_ld)
    ld0 = (s_ld, s_ld.v)
    do("gpsimd", lambda e: e.memset(ones64f[:], 1.0), inc=pg)
    do("gpsimd", lambda e: e.memset(ones16f[:], 1.0), waits=[pg], inc=pg)
    do("gpsimd", lambda e: e.memset(onesk1f[:], 1.0), waits=[pg], inc=pg)
    do("gpsimd", lambda e: e.memset(sc1[:], 0.0), waits=[pg], inc=pg)
    g_done = (pg, pg.v)

    wsrc = {"q": qw_ap, "k": kw_ap, "v": vw_ap, "o": ow_ap}
    wprev = None
    for n in "qkv":
        for dc in range(8):
            lw = dma("sync", lambda e, n=n, dc=dc: e.dma_start(out=wtmp[:, :],
                     in_=wsrc[n][dc * 128:(dc + 1) * 128, :]), s_w, waits=[wprev] if wprev else [])
            do("vector", lambda e: e.tensor_copy(out=ptok16[:, :], in_=wtmp[:, :]),
               waits=[lw, (pv, pv.v)] + ([wprev] if wprev else []), inc=pv)
            cast = (pv, pv.v)
            for g2 in range(2):
                def mtw(e, n=n, dc=dc, g2=g2):
                    for j in range(4):
                        ci = g2 * 4 + j
                        mm = e.transpose(out=PB[g2][:, j * 128:(j + 1) * 128],
                                         in_=ptok16[:, ci * 128:(ci + 1) * 128], identity=idb[:, :])
                    return mm
                do("tensor", mtw, waits=[cast, ld0, (ps_, ps_.v)], inc=pt)
                do("scalar", lambda e, n=n, dc=dc, g2=g2: e.copy(
                   out=wT[n][:, :].rearrange("p (ci x) -> p ci x", x=1024)[:, g2 * 4:(g2 + 1) * 4, dc * 128:(dc + 1) * 128],
                   in_=PB[g2][:, 0:512].rearrange("p (j x) -> p j x", x=128)),
                   waits=[(pt, pt.v)], inc=ps_)
            wprev = (ps_, ps_.v)
    wT_done = wprev
    for (n, col) in (("q", "a"), ("k", "b"), ("v", "b")):
        for ci in range(8):
            do("vector", lambda e, n=n, col=col, ci=ci: e.tensor_scalar(
               out=wT[n][:, ci * 1024:(ci + 1) * 1024], in0=wT[n][:, ci * 1024:(ci + 1) * 1024],
               scalar1=lncol[col][:, ci:ci + 1], scalar2=None, op0=ALU.mult),
               waits=[wT_done, (pv, pv.v)], inc=pv)
    ws_done = (pv, pv.v)
    sm_ld = []
    for (j, src) in ((0, lqw_ap), (1, lkw_ap), (2, lvw_ap)):
        sm_ld.append(dma("sync", lambda e, src=src: e.dma_start(out=wtmp[0:64, 0:64], in_=src[:, :]),
                     s_w, waits=[wT_done, (pv, pv.v)]))
        do("vector", lambda e: e.tensor_copy(out=ptok16[0:64, 0:64], in_=wtmp[0:64, 0:64]),
           waits=[sm_ld[-1], (pv, pv.v), (pt, pt.v)], inc=pv)
        do("tensor", lambda e: e.transpose(out=PB[0][0:64, 0:64], in_=ptok16[0:64, 0:64],
           identity=idb[0:64, 0:64]), waits=[(pv, pv.v), (ps_, ps_.v)], inc=pt)
        do("scalar", lambda e, j=j: e.copy(out=lsw[:, j * 64:(j + 1) * 64], in_=PB[0][0:64, 0:64]),
           waits=[(pt, pt.v)], inc=ps_)
    lsw_done = (ps_, ps_.v)
    for (n, src) in (("q", qb_ap), ("v", vb_ap)):
        l1 = dma("sync", lambda e, src=src: e.dma_start(out=brow[:, :],
                 in_=src[:].rearrange("(o d) -> o d", o=1)), s_w, waits=[(pv, pv.v)])
        for half in range(2):
            do("tensor", lambda e, half=half: e.matmul(out=F[5][:, :], lhsT=onesk1f[:, :],
               rhs=brow[:, half * 512:(half + 1) * 512], start=True, stop=True),
               waits=[l1, g_done, (pv, pv.v)], inc=pt)
            do("vector", lambda e, n=n, half=half: e.tensor_copy(out=bb[n][:, half * 512:(half + 1) * 512],
               in_=F[5][:, :]), waits=[(pt, pt.v)], inc=pv)
    do("tensor", lambda e: e.matmul(out=F[5][0:16, :], lhsT=onesk1f[:, 0:16], rhs=lvbrow8[:, :],
       start=True, stop=True), waits=[ld0, g_done, (pv, pv.v)], inc=pt)
    do("vector", lambda e: e.tensor_copy(out=lvbb[:, :], in_=F[5][0:16, :]), waits=[(pt, pt.v)], inc=pv)
    bias_done = (pv, pv.v)

    def tok_ln(out_t, in_t, marks):
        do("vector", lambda e: e.reduce_sum(out=stt_[:, 0:1], in_=in_t[:, :], axis=AX.X),
           waits=marks + [(pv, pv.v)], inc=pv)
        do("scalar", lambda e: e.activation(out=sqs[:, :], in_=in_t[:, :], func=AF.Square,
           accum_out=stt_[:, 1:2]), waits=marks + [(ps_, ps_.v)], inc=ps_)
        sqd = (ps_, ps_.v)
        do("vector", lambda e: e.tensor_scalar(out=stt_[:, 2:3], in0=stt_[:, 0:1], scalar1=1.0 / D,
           scalar2=None, op0=ALU.mult), waits=[(pv, pv.v), sqd], inc=pv)
        do("vector", lambda e: e.tensor_scalar(out=stt_[:, 3:4], in0=stt_[:, 1:2], scalar1=1.0 / D,
           scalar2=None, op0=ALU.mult), waits=[(pv, pv.v)], inc=pv)
        do("vector", lambda e: e.scalar_tensor_tensor(out=stt_[:, 4:5], in0=stt_[:, 2:3], scalar=1.0,
           in1=stt_[:, 2:3], op0=ALU.mult, op1=ALU.mult), waits=[(pv, pv.v)], inc=pv)
        do("vector", lambda e: e.tensor_sub(out=stt_[:, 3:4], in0=stt_[:, 3:4], in1=stt_[:, 4:5]),
           waits=[(pv, pv.v)], inc=pv)
        do("vector", lambda e: e.tensor_scalar(out=stt_[:, 3:4], in0=stt_[:, 3:4], scalar1=LNEPS,
           scalar2=None, op0=ALU.add), waits=[(pv, pv.v)], inc=pv)
        do("scalar", lambda e: e.activation(out=stt_[:, 5:6], in_=stt_[:, 3:4], func=AF.Sqrt),
           waits=[(pv, pv.v)], inc=ps_)
        do("vector", lambda e: e.reciprocal(out=stt_[:, 6:7], in_=stt_[:, 5:6]),
           waits=[(ps_, ps_.v)], inc=pv)
        do("vector", lambda e: e.tensor_scalar(out=out_t[:, :], in0=in_t[:, :], scalar1=stt_[:, 2:3],
           scalar2=stt_[:, 6:7], op0=ALU.subtract, op1=ALU.mult), waits=[(pv, pv.v)], inc=pv)
        return (pv, pv.v)

    ln_done = tok_ln(lnx, xall, [ld0])

    do("vector", lambda e: e.tensor_copy(out=ptok16[:, :], in_=lnx[:, :]), waits=[ln_done, (pv, pv.v)], inc=pv)
    lncast = (pv, pv.v)
    for g2 in range(2):
        def mtl(e, g2=g2):
            for j in range(4):
                ci = g2 * 4 + j
                mm = e.transpose(out=PB[g2][:, j * 128:(j + 1) * 128],
                                 in_=ptok16[:, ci * 128:(ci + 1) * 128], identity=idb[:, :])
            return mm
        do("tensor", mtl, waits=[lncast, (ps_, ps_.v)], inc=pt)
        do("scalar", lambda e, g2=g2: e.copy(out=lnxT[:, g2 * 512:(g2 + 1) * 512], in_=PB[g2][:, 0:512]),
           waits=[(pt, pt.v)], inc=ps_)
    lnT_done = (ps_, ps_.v)

    Hdst = {"q": qH, "k": kH, "v": vH}
    pr_done = {}
    for (n, bias) in (("q", "q"), ("k", None), ("v", "v")):
        for half in range(2):
            fb = half
            def mmp(e, n=n, half=half, fb=fb):
                for c in range(8):
                    mm = e.matmul(out=F[fb][:, :], lhsT=lnxT[:, c * 128:(c + 1) * 128],
                                  rhs=wT[n][:, c * 1024 + half * 512: c * 1024 + (half + 1) * 512],
                                  start=(c == 0), stop=(c == 7))
                return mm
            do("tensor", mmp, waits=[lnT_done, ws_done, (ps_, ps_.v), (pv, pv.v)], inc=pt)
            if bias:
                do("vector", lambda e, half=half, fb=fb, bias=bias: e.scalar_tensor_tensor(
                   out=ptok[:, half * 512:(half + 1) * 512], in0=F[fb][:, :], scalar=1.0,
                   in1=bb[bias][:, half * 512:(half + 1) * 512], op0=ALU.mult, op1=ALU.add),
                   waits=[(pt, pt.v), bias_done, (pv, pv.v)], inc=pv)
            else:
                do("vector", lambda e, half=half, fb=fb: e.tensor_copy(
                   out=ptok[:, half * 512:(half + 1) * 512], in_=F[fb][:, :]),
                   waits=[(pt, pt.v), (pv, pv.v)], inc=pv)
        do("vector", lambda e: e.tensor_copy(out=ptok16[:, :], in_=ptok[:, :]),
           waits=[(pv, pv.v)], inc=pv)
        pcast = (pv, pv.v)
        for g4 in range(4):
            def mth(e, g4=g4):
                for j in range(4):
                    h = g4 * 4 + j
                    mm = e.transpose(out=PB[g4 % 2][0:64, j * 128:(j + 1) * 128],
                                     in_=ptok16[:, h * 64:(h + 1) * 64], identity=idb[:, :])
                return mm
            do("tensor", mth, waits=[pcast, (ps_, ps_.v)], inc=pt)
            do("scalar", lambda e, n=n, g4=g4: e.copy(out=Hdst[n][:, g4 * 512:(g4 + 1) * 512],
               in_=PB[g4 % 2][0:64, 0:512]), waits=[(pt, pt.v)], inc=ps_)
        pr_done[n] = (ps_, ps_.v)

    def h_ln(out16, in32, scale_colap, marks):
        mk = list(marks)
        for q4 in range(4):
            do("tensor", lambda e, q4=q4, in32=in32: e.matmul(out=F[4][0:1, :], lhsT=ones64f[:, :],
               rhs=in32[:, q4 * 512:(q4 + 1) * 512], start=True, stop=True),
               waits=mk + [g_done, (pv, pv.v)], inc=pt)
            sum_mm = (pt, pt.v)
            do("scalar", lambda e, q4=q4, in32=in32: e.activation(out=sqs[0:64, 0:512],
               in_=in32[:, q4 * 512:(q4 + 1) * 512], func=AF.Square),
               waits=mk + [(ps_, ps_.v), (pv, pv.v)], inc=ps_)
            do("tensor", lambda e: e.matmul(out=F[5][0:1, :], lhsT=ones64f[:, :],
               rhs=sqs[0:64, 0:512], start=True, stop=True), waits=[(ps_, ps_.v)], inc=pt)
            ssq_mm = (pt, pt.v)
            do("vector", lambda e: e.tensor_scalar(out=ln64r[:, 0:512], in0=F[4][0:1, :],
               scalar1=1.0 / HD, scalar2=None, op0=ALU.mult), waits=[sum_mm, (pv, pv.v)], inc=pv)
            do("vector", lambda e: e.tensor_scalar(out=ln64r[:, 512:1024], in0=F[5][0:1, :],
               scalar1=1.0 / HD, scalar2=None, op0=ALU.mult), waits=[ssq_mm, (pv, pv.v)], inc=pv)
            do("vector", lambda e: e.scalar_tensor_tensor(out=ln64r[:, 1024:1536], in0=ln64r[:, 0:512],
               scalar=1.0, in1=ln64r[:, 0:512], op0=ALU.mult, op1=ALU.mult), waits=[(pv, pv.v)], inc=pv)
            do("vector", lambda e: e.tensor_sub(out=ln64r[:, 512:1024], in0=ln64r[:, 512:1024],
               in1=ln64r[:, 1024:1536]), waits=[(pv, pv.v)], inc=pv)
            do("vector", lambda e: e.tensor_scalar(out=ln64r[:, 512:1024], in0=ln64r[:, 512:1024],
               scalar1=LNEPS, scalar2=None, op0=ALU.add), waits=[(pv, pv.v)], inc=pv)
            do("scalar", lambda e: e.activation(out=ln64r[:, 1536:2048], in_=ln64r[:, 512:1024],
               func=AF.Sqrt), waits=[(pv, pv.v)], inc=ps_)
            do("vector", lambda e: e.reciprocal(out=ln64r[:, 1024:1536], in_=ln64r[:, 1536:2048]),
               waits=[(ps_, ps_.v)], inc=pv)
            rows_done = (pv, pv.v)
            do("tensor", lambda e: e.matmul(out=F[4][0:64, :], lhsT=onesk1f[:, 0:64],
               rhs=ln64r[:, 0:512], start=True, stop=True), waits=[rows_done], inc=pt)
            mb = (pt, pt.v)
            do("tensor", lambda e: e.matmul(out=F[5][0:64, :], lhsT=onesk1f[:, 0:64],
               rhs=ln64r[:, 1024:1536], start=True, stop=True), waits=[rows_done], inc=pt)
            rb = (pt, pt.v)
            do("vector", lambda e, q4=q4, in32=in32: e.tensor_sub(out=sqs[0:64, 0:512],
               in0=in32[:, q4 * 512:(q4 + 1) * 512], in1=F[4][0:64, :]), waits=[mb, (pv, pv.v)], inc=pv)
            do("vector", lambda e: e.tensor_tensor(out=sqs[0:64, 0:512], in0=sqs[0:64, 0:512],
               in1=F[5][0:64, :], op=ALU.mult), waits=[rb, (pv, pv.v)], inc=pv)
            do("vector", lambda e, q4=q4, out16=out16, scale_colap=scale_colap: e.tensor_scalar(
               out=out16[:, q4 * 512:(q4 + 1) * 512], in0=sqs[0:64, 0:512],
               scalar1=scale_colap, scalar2=None, op0=ALU.mult), waits=[(pv, pv.v)], inc=pv)
        return (pv, pv.v)

    for q4 in range(4):
        do("tensor", lambda e, q4=q4: e.matmul(out=F[q4 % 2][0:64, :], lhsT=lsw[:, 64:128],
           rhs=kH[:, q4 * 512:(q4 + 1) * 512], start=True, stop=True),
           waits=[pr_done["k"], lsw_done, (ps_, ps_.v)], inc=pt)
        do("scalar", lambda e, q4=q4: e.activation(out=knr[:, q4 * 512:(q4 + 1) * 512],
           in_=F[q4 % 2][0:64, :], func=AF.Identity, bias=lsb[:, 1:2]), waits=[(pt, pt.v)], inc=ps_)
    kn_raw = (ps_, ps_.v)
    kn_done = h_ln(kn16, knr, lsb[:, 3:4], [kn_raw])
    qitH = knr

    vi_ev = None
    for grp in range(16):
        def mmv(e, grp=grp):
            for j in range(8):
                bh = grp * 8 + j
                mm = e.matmul(out=F[grp % 2][0:16, j * 64:(j + 1) * 64],
                              lhsT=vH[:, bh * 16:(bh + 1) * 16], rhs=lsw[:, 128:192],
                              start=True, stop=True)
            return mm
        mw = [pr_done["v"], lsw_done]
        if vi_ev:
            mw.append(vi_ev)
        do("tensor", mmv, waits=mw + [(ps_, ps_.v)], inc=pt)
        do("vector", lambda e, grp=grp: e.scalar_tensor_tensor(out=viTT[:, grp * 512:(grp + 1) * 512],
           in0=F[grp % 2][0:16, :], scalar=1.0,
           in1=lvbb[:, :].rearrange("p (j x) -> p j x", x=64).broadcast_to([16, 8, 64]) if False else lvbb[:, :],
           op0=ALU.mult, op1=ALU.add), waits=[(pt, pt.v), bias_done, (pv, pv.v)], inc=pv)
        vi_ev = (pv, pv.v)
    vi_done = vi_ev

    do("vector", lambda e: e.tensor_copy(out=qcurH[:, :], in_=qH[:, :]), waits=[pr_done["q"], (pv, pv.v)], inc=pv)
    qc_done = (pv, pv.v)
    qcur16 = k.sb(f"{U}_qcur16", [64, H * 128], BF16)

    wprev2 = None
    for dc in range(8):
        lw = dma("sync", lambda e, dc=dc: e.dma_start(out=wtmp[:, :],
                 in_=ow_ap[dc * 128:(dc + 1) * 128, :]), s_w,
                 waits=[wprev2] if wprev2 else [pr_done["v"], (pv, pv.v)])
        do("vector", lambda e: e.tensor_copy(out=ptok16[:, :], in_=wtmp[:, :]),
           waits=[lw, (pv, pv.v), (pt, pt.v)], inc=pv)
        cast2 = (pv, pv.v)
        for g2 in range(2):
            def mtw2(e, dc=dc, g2=g2):
                for j in range(4):
                    ci = g2 * 4 + j
                    mm = e.transpose(out=PB[g2][:, j * 128:(j + 1) * 128],
                                     in_=ptok16[:, ci * 128:(ci + 1) * 128], identity=idb[:, :])
                return mm
            do("tensor", mtw2, waits=[cast2, (ps_, ps_.v)], inc=pt)
            do("scalar", lambda e, dc=dc, g2=g2: e.copy(
               out=wT["o"][:, :].rearrange("p (ci x) -> p ci x", x=1024)[:, g2 * 4:(g2 + 1) * 4, dc * 128:(dc + 1) * 128],
               in_=PB[g2][:, 0:512].rearrange("p (j x) -> p j x", x=128)),
               waits=[(pt, pt.v)], inc=ps_)
        wprev2 = (ps_, ps_.v)
    l1o = dma("sync", lambda e: e.dma_start(out=brow[:, :],
              in_=ob_ap[:].rearrange("(o d) -> o d", o=1)), s_w, waits=[wprev2])
    for half in range(2):
        do("tensor", lambda e, half=half: e.matmul(out=F[5][:, :], lhsT=onesk1f[:, :],
           rhs=brow[:, half * 512:(half + 1) * 512], start=True, stop=True),
           waits=[l1o, (pv, pv.v)], inc=pt)
        do("vector", lambda e, half=half: e.tensor_copy(out=bb["o"][:, half * 512:(half + 1) * 512],
           in_=F[5][:, :]), waits=[(pt, pt.v), (pv, pv.v)], inc=pv)
    obb_done = (pv, pv.v)
    ws_done = (ps_, ps_.v)
    bias_done = obb_done

    last_qit = None
    for it in range(MAX_IT):
        do("vector", lambda e: e.tensor_copy(out=qcur16[:, :], in_=qcurH[:, :]),
           waits=[qc_done, (pv, pv.v)], inc=pv)
        qcc = (pv, pv.v)
        for q4 in range(4):
            do("tensor", lambda e, q4=q4: e.matmul(out=F[q4 % 2][0:64, :], lhsT=lsw[:, 0:64],
               rhs=qcur16[:, q4 * 512:(q4 + 1) * 512], start=True, stop=True),
               waits=[qcc, lsw_done, (ps_, ps_.v), (pv, pv.v)], inc=pt)
            do("scalar", lambda e, q4=q4: e.activation(out=knr[:, q4 * 512:(q4 + 1) * 512],
               in_=F[q4 % 2][0:64, :], func=AF.Identity, bias=lsb[:, 0:1]),
               waits=[(pt, pt.v)], inc=ps_)
        qn_raw = (ps_, ps_.v)
        qn_done = h_ln(qn16, knr, lsb[:, 2:3], [qn_raw])
        lg_ev = {}
        for bank in range(4):
            def mml(e, bank=bank):
                for j in range(32):
                    bh = bank * 32 + j
                    mm = e.matmul(out=F[bank][0:16, j * 16:(j + 1) * 16],
                                  lhsT=qn16[:, bh * 16:(bh + 1) * 16],
                                  rhs=kn16[:, bh * 16:(bh + 1) * 16], start=True, stop=True)
                return mm
            do("tensor", mml, waits=[qn_done, kn_done, (ps_, ps_.v)], inc=pt)
            do("scalar", lambda e, bank=bank: e.activation(out=lgR[:, bank * 512:(bank + 1) * 512],
               in_=F[bank][0:16, :], func=AF.Identity, scale=0.125), waits=[(pt, pt.v)], inc=ps_)
            lg_ev[bank] = (ps_, ps_.v)
        lg_done = (ps_, ps_.v)
        do("vector", lambda e: e.tensor_reduce(out=mxr[:, :],
           in_=lgR[:, :].rearrange("p (b k) -> p b k", k=16), axis=AX.X, op=ALU.max),
           waits=[lg_done, (pv, pv.v)], inc=pv)
        do("vector", lambda e: e.tensor_tensor(out=lgR[:, :].rearrange("p (b k) -> p b k", k=16),
           in0=lgR[:, :].rearrange("p (b k) -> p b k", k=16),
           in1=mxr[:, :].broadcast_to([16, NB, 16]), op=ALU.subtract), waits=[(pv, pv.v)], inc=pv)
        sm1 = (pv, pv.v)
        do("scalar", lambda e: e.activation(out=attnR[:, :], in_=lgR[:, :], func=AF.Exp),
           waits=[sm1, (ps_, ps_.v)], inc=ps_)
        do("vector", lambda e: e.tensor_reduce(out=smr[:, :],
           in_=attnR[:, :].rearrange("p (b k) -> p b k", k=16), axis=AX.X, op=ALU.add),
           waits=[(ps_, ps_.v), (pv, pv.v)], inc=pv)
        do("vector", lambda e: e.reciprocal(out=smr[:, :], in_=smr[:, :]), waits=[(pv, pv.v)], inc=pv)
        do("vector", lambda e: e.tensor_tensor(out=attnR[:, :].rearrange("p (b k) -> p b k", k=16),
           in0=attnR[:, :].rearrange("p (b k) -> p b k", k=16),
           in1=smr[:, :].broadcast_to([16, NB, 16]), op=ALU.mult), waits=[(pv, pv.v)], inc=pv)
        sm_done = (pv, pv.v)
        for g2 in range(2):
            def mta(e, g2=g2):
                for j in range(64):
                    bh = g2 * 64 + j
                    mm = e.transpose(out=PB[g2][0:16, j * 16:(j + 1) * 16],
                                     in_=attnR[:, bh * 16:(bh + 1) * 16], identity=idb[0:16, 0:16])
                return mm
            do("tensor", mta, waits=[sm_done, (ps_, ps_.v)], inc=pt)
            do("scalar", lambda e, g2=g2: e.copy(out=attnT16[:, g2 * 1024:(g2 + 1) * 1024],
               in_=PB[g2][0:16, 0:1024]), waits=[(pt, pt.v)], inc=ps_)
        at_done = (ps_, ps_.v)
        it_ev = None
        for grp in range(16):
            def mmi(e, grp=grp):
                for j in range(8):
                    bh = grp * 8 + j
                    mm = e.matmul(out=F[grp % 2][0:16, j * 64:(j + 1) * 64],
                                  lhsT=attnT16[:, bh * 16:(bh + 1) * 16],
                                  rhs=viTT[:, bh * 64:(bh + 1) * 64], start=True, stop=True)
                return mm
            mw = [at_done, vi_done, (ps_, ps_.v)]
            do("tensor", mmi, waits=mw, inc=pt)
            do("scalar", lambda e, grp=grp: e.copy(out=iterR[:, grp * 512:(grp + 1) * 512],
               in_=F[grp % 2][0:16, :]), waits=[(pt, pt.v)], inc=ps_)
            it_ev = (ps_, ps_.v)
        iter_done = it_ev
        if it < MAX_IT - 1:
            for g2 in range(2):
                def mtq(e, g2=g2):
                    for j in range(64):
                        bh = g2 * 64 + j
                        mm = e.transpose(out=PB[g2][0:64, j * 16:(j + 1) * 16],
                                         in_=iterR[:, bh * 64:(bh + 1) * 64], identity=idb[0:16, 0:16])
                    return mm
                do("tensor", mtq, waits=[iter_done, (ps_, ps_.v)], inc=pt)
                do("vector", lambda e, g2=g2: e.tensor_copy(out=qitH[:, g2 * 1024:(g2 + 1) * 1024],
                   in_=PB[g2][0:64, 0:1024]), waits=[(pt, pt.v), (pv, pv.v)], inc=pv)
            last_qit = (pv, pv.v)
        if it == 0:
            do("vector", lambda e: e.tensor_copy(out=aoR[:, :], in_=iterR[:, :]),
               waits=[iter_done, (pv, pv.v)], inc=pv)
            do("vector", lambda e: e.tensor_add(out=qcurH[:, :], in0=qcurH[:, :], in1=qitH[:, :]),
               waits=[last_qit, (pv, pv.v)], inc=pv)
            qc_done = (pv, pv.v)
        else:
            for ch in range(16):
                do("vector", lambda e, ch=ch: e.tensor_sub(out=sqs[0:16, 0:512],
                   in0=iterR[:, ch * 512:(ch + 1) * 512], in1=aoR[:, ch * 512:(ch + 1) * 512]),
                   waits=[iter_done, (pv, pv.v)], inc=pv)
                do("vector", lambda e, ch=ch: e.tensor_reduce(out=dsc[:, ch:ch + 1], in_=sqs[0:16, 0:512],
                   axis=AX.X, op=ALU.add, apply_absolute_value=True), waits=[(pv, pv.v)], inc=pv)
            do("vector", lambda e: e.tensor_reduce(out=dsc[:, 16:17], in_=dsc[:, 0:16], axis=AX.X,
               op=ALU.add), waits=[(pv, pv.v)], inc=pv)
            dred = (pv, pv.v)
            do("tensor", lambda e: e.matmul(out=F[4][0:1, 0:1], lhsT=dsc[:, 16:17], rhs=ones16f[:, :],
               start=True, stop=True), waits=[dred, g_done], inc=pt)
            do("vector", lambda e: e.tensor_scalar(out=sc1[:, 1:2], in0=F[4][0:1, 0:1],
               scalar1=float(DIFF_SCALE), scalar2=None, op0=ALU.mult), waits=[(pt, pt.v), (pv, pv.v)], inc=pv)
            do("vector", lambda e: e.tensor_scalar(out=sc1[:, 5:6], in0=sc1[:, 1:2], scalar1=float(FACTOR),
               scalar2=float(THRESH), op0=ALU.mult, op1=ALU.add), waits=[(pv, pv.v)], inc=pv)
            do("vector", lambda e: e.tensor_tensor(out=sc1[:, 2:3], in0=sc1[:, 1:2], in1=sc1[:, 5:6],
               op=ALU.is_lt), waits=[(pv, pv.v)], inc=pv)
            do("vector", lambda e: e.tensor_scalar(out=sc1[:, 3:4], in0=sc1[:, 0:1], scalar1=-1.0,
               scalar2=1.0, op0=ALU.mult, op1=ALU.add), waits=[(pv, pv.v)], inc=pv)
            do("vector", lambda e: e.tensor_tensor(out=sc1[:, 2:3], in0=sc1[:, 2:3], in1=sc1[:, 3:4],
               op=ALU.mult), waits=[(pv, pv.v)], inc=pv)
            do("vector", lambda e: e.tensor_sub(out=sc1[:, 4:5], in0=sc1[:, 3:4], in1=sc1[:, 2:3]),
               waits=[(pv, pv.v)], inc=pv)
            do("vector", lambda e: e.tensor_add(out=sc1[:, 0:1], in0=sc1[:, 0:1], in1=sc1[:, 2:3]),
               waits=[(pv, pv.v)], inc=pv)
            scal_done = (pv, pv.v)
            do("tensor", lambda e: e.matmul(out=F[4][0:16, 0:1], lhsT=onesk1f[:, 0:16],
               rhs=sc1[:, 3:4], start=True, stop=True), waits=[scal_done], inc=pt)
            do("vector", lambda e: e.tensor_copy(out=g16[:, 0:1], in_=F[4][0:16, 0:1]),
               waits=[(pt, pt.v), (pv, pv.v)], inc=pv)
            do("tensor", lambda e: e.matmul(out=F[5][0:64, 0:1], lhsT=onesk1f[:, 0:64],
               rhs=sc1[:, 4:5], start=True, stop=True), waits=[scal_done], inc=pt)
            do("vector", lambda e: e.tensor_copy(out=g64[:, 0:1], in_=F[5][0:64, 0:1]),
               waits=[(pt, pt.v), (pv, pv.v)], inc=pv)
            gb_done = (pv, pv.v)
            do("vector", lambda e: e.tensor_sub(out=iterR[:, :], in0=iterR[:, :], in1=aoR[:, :]),
               waits=[gb_done] + ([last_qit] if last_qit else []), inc=pv)
            do("vector", lambda e: e.scalar_tensor_tensor(out=aoR[:, :], in0=iterR[:, :],
               scalar=g16[:, 0:1], in1=aoR[:, :], op0=ALU.mult, op1=ALU.add),
               waits=[(pv, pv.v)], inc=pv)
            if it < MAX_IT - 1:
                do("vector", lambda e: e.scalar_tensor_tensor(out=qcurH[:, :], in0=qitH[:, :],
                   scalar=g64[:, 0:1], in1=qcurH[:, :], op0=ALU.mult, op1=ALU.add),
                   waits=[(pv, pv.v)], inc=pv)
            qc_done = (pv, pv.v)
    ao_final = (pv, pv.v)

    def mtf(e):
        for bh in range(128):
            h, b = bh // 8, bh % 8
            mm = e.transpose(out=PB[0][(h % 2) * 64:(h % 2) * 64 + 64,
                                       (h // 2) * 128 + b * 16:(h // 2) * 128 + b * 16 + 16],
                             in_=aoR[:, bh * 64:(bh + 1) * 64], identity=idb[0:16, 0:16],
                             tile_position=(0, (h % 2) * 64))
        return mm
    do("tensor", mtf, waits=[ao_final, (ps_, ps_.v)], inc=pt)
    do("scalar", lambda e: e.copy(out=aF16[:, :], in_=PB[0][:, :]), waits=[(pt, pt.v)], inc=ps_)
    aF_done = (ps_, ps_.v)
    for half in range(2):
        def mmo(e, half=half):
            for c in range(8):
                mm = e.matmul(out=F[half][:, :], lhsT=aF16[:, c * 128:(c + 1) * 128],
                              rhs=wT["o"][:, c * 1024 + half * 512: c * 1024 + (half + 1) * 512],
                              start=(c == 0), stop=(c == 7))
            return mm
        do("tensor", mmo, waits=[aF_done, ws_done], inc=pt)
        do("vector", lambda e, half=half: e.scalar_tensor_tensor(
           out=s1t[:, half * 512:(half + 1) * 512], in0=F[half][:, :], scalar=1.0,
           in1=bb["o"][:, half * 512:(half + 1) * 512], op0=ALU.mult, op1=ALU.add),
           waits=[(pt, pt.v), bias_done, (pv, pv.v)], inc=pv)
    s1_done = (pv, pv.v)
    dma("gpsimd", lambda e: e.dma_start(out=s1_ap[:, :], in_=s1t[:, :]), s_st, waits=[s1_done])
    do("gpsimd", None, waits=[(s_st, s_st.v)])


def ph_final(k: KB, T, x1_ap, g1_ap, s1own_ap, ob_ap, y_ap, uniq="fin"):
    nc, do, dma = k.nc, k.do, k.dmago
    D = 1024
    nt = T // 128
    pv, ps_, pt, pg = k.pg["vector"], k.pg["scalar"], k.pg["tensor"], k.pg["gpsimd"]
    U = uniq
    x1t = [k.sb(f"{U}_x1{p}", [128, D], F32) for p in range(2)]
    g1t = [k.sb(f"{U}_g1{p}", [128, D], BF16) for p in range(2)]
    st = k.sb(f"{U}_st", [128, D], F32)
    stob = k.sb(f"{U}_stob", [128, D], F32)
    s16 = k.sb(f"{U}_s16o", [16, D], F32)
    brow = k.sb(f"{U}_brow", [1, D], F32)
    yt = [k.sb(f"{U}_yt{p}", [128, D], F32) for p in range(2)]
    onesk1f = k.sb(f"{U}_onesk1f", [1, 128], F32)
    F = [k.ps(f"{U}_F{i}", [128, 512], F32) for i in range(2)]
    s_ld = [k.sem(f"{U}_sld{p}") for p in range(2)]
    s_c = k.sem(f"{U}_sc")
    s_st = [k.sem(f"{U}_sstq{p}") for p in range(2)]

    do("gpsimd", lambda e: e.memset(onesk1f[:], 1.0), inc=pg)
    g_done = (pg, pg.v)
    lb = dma("sync", lambda e: e.dma_start(out=brow[:, :], in_=ob_ap[:].rearrange("(o d) -> o d", o=1)), s_c)
    l16 = dma("sync", lambda e: e.dma_start(out=s16[:, :], in_=s1own_ap[:, :]), s_c)
    for half in range(2):
        do("tensor", lambda e, half=half: e.matmul(out=F[half][:, :], lhsT=onesk1f[:, :],
           rhs=brow[:, half * 512:(half + 1) * 512], start=True, stop=True),
           waits=[lb, g_done], inc=pt)
        do("vector", lambda e, half=half: e.tensor_copy(out=stob[:, half * 512:(half + 1) * 512],
           in_=F[half][:, :]), waits=[(pt, pt.v)], inc=pv)
        do("vector", lambda e, half=half: e.tensor_copy(out=st[:, half * 512:(half + 1) * 512],
           in_=stob[:, half * 512:(half + 1) * 512]), waits=[(pv, pv.v)], inc=pv)
    do("vector", lambda e: e.tensor_copy(out=st[0:16, :], in_=s16[:, :]), waits=[l16, (pv, pv.v)], inc=pv)
    s_done0 = (pv, pv.v)

    tile_done = {}
    for i in range(nt):
        p = i % 2
        tw = [tile_done[i - 2]] if i >= 2 else []
        dma("sync", lambda e, i=i, p=p: e.dma_start(out=x1t[p][:, :], in_=x1_ap[i * 128:(i + 1) * 128, :]),
            s_ld[p], waits=tw)
        dma("sync", lambda e, i=i, p=p: e.dma_start(out=g1t[p][:, :], in_=g1_ap[i * 128:(i + 1) * 128, :]),
            s_ld[p])
        ld = (s_ld[p], s_ld[p].v)
        sv = st[:, :] if i > 0 else st[:, :]
        def comb(e, i=i, p=p):
            src_s = st if i == 0 else stob
            return e.tensor_tensor(out=yt[p][:, :], in0=g1t[p][:, :], in1=src_s[:, :], op=ALU.mult)
        if i == 0:
            do("vector", comb, waits=[ld, s_done0, (pv, pv.v)], inc=pv)
        else:
            do("vector", comb, waits=[ld, (pv, pv.v)], inc=pv)
        m = (pv, pv.v)
        do("vector", lambda e, p=p: e.tensor_add(out=yt[p][:, :], in0=yt[p][:, :], in1=x1t[p][:, :]),
           waits=[m], inc=pv)
        ydone = (pv, pv.v)
        dma("gpsimd", lambda e, i=i, p=p: e.dma_start(out=y_ap[i * 128:(i + 1) * 128, :], in_=yt[p][:, :]),
            s_st[p], waits=[ydone])
        tile_done[i] = (pv, pv.v)
    do("gpsimd", None, waits=[(s_st[0], s_st[0].v), (s_st[1], s_st[1].v)])



def ph_moe_apply_c(k: KB, T, xnfm16_ap, xlist_ap, B, w1s_ap, b1s_ap,
                   w2s_ap, b2s_ap, fw1_ap, fb1_ap, fw2_ap, fb2_ap, gates_ap, yacc_ap,
                   uniq="mc"):
    from concourse.bass_core_rust import IndirectOffsetOnAxis
    nc, do, dma = k.nc, k.do, k.dmago
    D, DFF = 1024, 4096
    TT = 512
    pv, ps_, pt, pg = k.pg["vector"], k.pg["scalar"], k.pg["tensor"], k.pg["gpsimd"]
    U = uniq

    w116 = k.sb(f"{U}_w116", [128, 8 * DFF], BF16)
    w216 = k.sb(f"{U}_w216", [128, 32 * D], BF16)
    bcol1 = k.sb(f"{U}_bc1", [128, 32], F32)
    b2row = k.sb(f"{U}_b2row", [1, D], F32)
    b2b = k.sb(f"{U}_b2b", [128, D], F32)
    xn16 = [k.sb(f"{U}_xn16{p}", [128, 8 * TT], BF16) for p in range(2)]
    xg = [k.sb(f"{U}_xg{p}", [128, 1028], BF16) for p in range(2)]
    idxg = [k.sb(f"{U}_idxg{p}", [128, 4], I32) for p in range(2)]
    gg = [k.sb(f"{U}_gg{p}", [128, 4], F32) for p in range(2)]
    rcf = k.sb(f"{U}_rcf", [128, 1], F32)
    y116_t = k.sb(f"{U}_y116", [128, 32 * TT], BF16)
    y116 = [y116_t, y116_t]
    yst = k.sb(f"{U}_yst", [128, D], F32)
    gcols = k.sb(f"{U}_gcols", [128, T // 128], F32)
    onesk1f = k.sb(f"{U}_onesk1f", [1, 128], F32)
    idb = k.sb(f"{U}_idb", [128, 128], BF16)
    F = [k.ps(f"{U}_F{i}", [128, 512], F32) for i in range(5)]
    PBt = k.ps(f"{U}_PBt", [128, 1024], BF16)
    c_idb = k.const(f"{U}_idbc", np.eye(128, dtype=ml_dtypes.bfloat16))

    s_w = k.sem(f"{U}_sw")
    s_w1 = k.sem(f"{U}_sw1")
    s_x = [k.sem(f"{U}_sx{p}") for p in range(2)]
    s_g = k.sem(f"{U}_sg")
    s_gi = [k.sem(f"{U}_sgi{p}") for p in range(2)]
    s_st = k.sem(f"{U}_sst")

    do("gpsimd", lambda e: e.memset(onesk1f[:], 1.0), inc=pg)
    ones_done = (pg, pg.v)
    dma("sync", lambda e: e.dma_start(out=idb[:, :], in_=c_idb[:, :]), s_g)
    idb_done = (s_g, s_g.v)

    NPASS = 9
    pass_store_done = None
    wl_prev = None
    w116_free = None
    w216_free = None
    w1ev_prev = None
    y116_rd = {}
    bias_free = None
    for pi in range(NPASS):
        if pi == 0:
            w1src, b1src, w2src, b2src = fw1_ap, fb1_ap, fw2_ap, fb2_ap
            NTp = T // TT
        else:
            ee = pi - 1
            w1src, b1src, w2src, b2src = (w1s_ap[ee], b1s_ap[ee], w2s_ap[ee], b2s_ap[ee])
            NTp = B // TT
        wl = []
        for c in range(8):
            ww = []
            if c == 0 and w116_free is not None:
                ww = [w116_free]
            wl.append(dma("gpsimd", lambda e, c=c, w1src=w1src: e.dma_start(
                out=w116[:, c * DFF:(c + 1) * DFF], in_=w1src[c * 128:(c + 1) * 128, :]),
                s_w1, waits=ww, tag=f"w1_{pi}_{c}"))
        w1_done = wl[-1]
        for c in range(32):
            ww = []
            if c == 0 and w216_free is not None:
                ww = [w216_free]
            wl.append(dma("gpsimd", lambda e, c=c, w2src=w2src: e.dma_start(
                out=w216[:, c * D:(c + 1) * D], in_=w2src[c * 128:(c + 1) * 128, :]),
                s_w, waits=ww, tag=f"w2_{pi}_{c}"))
        for ci, c in enumerate(range(32)):
            dma("sync", lambda e, c=c, b1src=b1src: e.dma_start(out=bcol1[:, c:c + 1],
                in_=b1src[c * 128:(c + 1) * 128]), s_g,
                waits=[bias_free] if (ci == 0 and bias_free) else [])
        dma("sync", lambda e, b2src=b2src: e.dma_start(out=b2row[:, :],
            in_=b2src[:].rearrange("(o d) -> o d", o=1)), s_g)
        if pi == 0:
            dma("sync", lambda e: e.dma_start(out=gcols[:, :], in_=gates_ap[0, :, :]), s_g)
        small_ld = (s_g, s_g.v)
        w_done = wl[-1]
        bb = []
        for half in range(2):
            do("tensor", lambda e, half=half: e.matmul(out=F[4][:, :], lhsT=onesk1f[:, :],
               rhs=b2row[:, half * 512:(half + 1) * 512], start=True, stop=True),
               waits=[small_ld, ones_done] + ([bb[-1]] if bb else []), inc=pt)
            do("vector", lambda e, half=half: e.tensor_copy(out=b2b[:, half * 512:(half + 1) * 512],
               in_=F[4][:, :]), waits=[(pt, pt.v)], inc=pv)
            bb.append((pv, pv.v))
        bcast_done = bb[-1]

        tile_done = {}
        for tt in range(NTp):
            p = tt % 2
            tw = [tile_done[tt - 2]] if tt >= 2 else []
            if tt < 2 and wl_prev is not None and pi > 0:
                tw = tw + [wl_prev]
            if pi == 0:
                lx = None
                for c in range(8):
                    lx = dma("sync", lambda e, tt=tt, c=c, p=p: e.dma_start(
                        out=xn16[p][:, c * TT:(c + 1) * TT],
                        in_=xnfm16_ap[c * 128:(c + 1) * 128, tt * TT:(tt + 1) * TT]), s_x[p],
                        waits=tw if c == 0 else [])
                in_ready = lx
            else:
                ee = pi - 1
                for sub in range(4):
                    r0 = tt * TT + sub * 128
                    sp = sub % 2
                    lig = dma("sync", lambda e, ee=ee, r0=r0, sp=sp: e.dma_start(out=xg[sp][:, :],
                              in_=xlist_ap[ee][r0:r0 + 128, :]), s_x[p],
                              waits=(tw if sub == 0 else []) + [(pt, pt.v)])
                    do("vector", lambda e, p=p, sub=sub, sp=sp: e.scalar_tensor_tensor(out=rcf[:, :],
                       in0=xg[sp][:, 1024:1025], scalar=64.0, in1=xg[sp][:, 1025:1026],
                       op0=ALU.mult, op1=ALU.add), waits=[lig, (pv, pv.v), (s_st, s_st.v)], inc=pv)
                    do("vector", lambda e, p=p, sub=sub: e.tensor_copy(out=idxg[p][:, sub:sub + 1],
                       in_=rcf[:, :]), waits=[(pv, pv.v)], inc=pv)
                    do("vector", lambda e, p=p, sub=sub, sp=sp: e.tensor_tensor(out=gg[p][:, sub:sub + 1],
                       in0=xg[sp][:, 1026:1027], in1=xg[sp][:, 1027:1028], op=ALU.add),
                       waits=[(pv, pv.v)], inc=pv)
                    def mtx(e, sub=sub, sp=sp):
                        for c in range(8):
                            mm = e.transpose(out=PBt[:, c * 128:(c + 1) * 128],
                                             in_=xg[sp][:, c * 128:(c + 1) * 128], identity=idb[:, :])
                        return mm
                    do("tensor", mtx, waits=[(pv, pv.v), idb_done, (ps_, ps_.v)], inc=pt)
                    def mev(e, p=p, sub=sub):
                        return e.copy(out=xn16[p][:, :].rearrange("q (c x) -> q c x", x=TT)[:, :, sub * 128:(sub + 1) * 128],
                                      in_=PBt[:, :].rearrange("q (c x) -> q c x", x=128))
                    do("scalar", mev, waits=[(pt, pt.v)], inc=ps_)
                in_ready = (ps_, ps_.v)
            ev = {}
            for ff in range(32):
                fbk = ff % 2
                mw = [in_ready, w1_done]
                if ff >= 2:
                    mw.append(ev[ff - 2])
                elif w1ev_prev is not None:
                    mw.append(w1ev_prev)
                def mm1(e, ff=ff, fbk=fbk, p=p):
                    for c in range(8):
                        mm = e.matmul(out=F[fbk][:, :], lhsT=w116[:, c * DFF + ff * 128: c * DFF + (ff + 1) * 128],
                                      rhs=xn16[p][:, c * TT:(c + 1) * TT], start=(c == 0), stop=(c == 7))
                    return mm
                do("tensor", mm1, waits=mw, inc=pt)
                evw = [(pt, pt.v), small_ld]
                if ff < 2 and y116_rd.get("last") is not None:
                    evw.append(y116_rd["last"])
                do("scalar", lambda e, ff=ff, fbk=fbk, p=p: e.activation(out=y116[p][:, ff * TT:(ff + 1) * TT],
                   in_=F[fbk][:, :], func=AF.Gelu, bias=bcol1[:, ff:ff + 1]), waits=evw, inc=ps_)
                ev[ff] = (ps_, ps_.v)
            y1_done = (ps_, ps_.v)
            w1ev_prev = (pt, pt.v)
            if tt == NTp - 1:
                w116_free = (pt, pt.v)
                bias_free = (ps_, ps_.v)
            for sub in range(TT // 128):
                for half in range(2):
                    fbk = 2 + (sub * 2 + half) % 2
                    mw = [y1_done, w_done]
                    prev_it = sub * 2 + half - 2
                    if prev_it >= 0:
                        mw.append(tile_done.get((tt, prev_it), (pv, pv.v)))
                    def mm2(e, sub=sub, half=half, fbk=fbk, p=p):
                        for c in range(32):
                            mm = e.matmul(out=F[fbk][:, :],
                                          lhsT=y116[p][:, c * TT + sub * 128: c * TT + (sub + 1) * 128],
                                          rhs=w216[:, c * D + half * 512: c * D + (half + 1) * 512],
                                          start=(c == 0), stop=(c == 31))
                        return mm
                    do("tensor", mm2, waits=mw, inc=pt)
                    if tt == NTp - 1 and sub == TT // 128 - 1 and half == 1:
                        w216_free = (pt, pt.v)
                    if sub == TT // 128 - 1 and half == 1:
                        y116_rd["last"] = (pt, pt.v)
                    do("vector", lambda e, half=half, fbk=fbk: e.scalar_tensor_tensor(
                       out=yst[:, half * 512:(half + 1) * 512], in0=F[fbk][:, :], scalar=1.0,
                       in1=b2b[:, half * 512:(half + 1) * 512], op0=ALU.mult, op1=ALU.add),
                       waits=[(pt, pt.v), bcast_done, (pv, pv.v), (s_st, s_st.v)], inc=pv)
                    if pi == 0:
                        gi = tt * (TT // 128) + sub
                        do("vector", lambda e, half=half, gi=gi: e.tensor_scalar(
                           out=yst[:, half * 512:(half + 1) * 512], in0=yst[:, half * 512:(half + 1) * 512],
                           scalar1=gcols[:, gi:gi + 1], scalar2=None, op0=ALU.mult),
                           waits=[(pv, pv.v)], inc=pv)
                    else:
                        do("vector", lambda e, half=half, p=p, sub=sub: e.tensor_scalar(
                           out=yst[:, half * 512:(half + 1) * 512], in0=yst[:, half * 512:(half + 1) * 512],
                           scalar1=gg[p][:, sub:sub + 1], scalar2=None, op0=ALU.mult),
                           waits=[(pv, pv.v)], inc=pv)
                    tile_done[(tt, sub * 2 + half)] = (pv, pv.v)
                stw = [tile_done[(tt, sub * 2)], tile_done[(tt, sub * 2 + 1)]]
                if pass_store_done is not None:
                    stw.append(pass_store_done)
                if pi == 0:
                    dma("gpsimd", lambda e, tt=tt, sub=sub: e.dma_start(
                        out=yacc_ap[tt * TT + sub * 128: tt * TT + (sub + 1) * 128, :], in_=yst[:, :]),
                        s_st, waits=stw, tag=f"st_{pi}_{tt}_{sub}")
                else:
                    dma("gpsimd", lambda e, p=p, sub=sub: e.indirect_dma_start(
                        out=yacc_ap[:, :], out_offset=IndirectOffsetOnAxis(idxg[p][:, sub:sub + 1], 0),
                        in_=yst[:, :], in_offset=None, compute_op=ALU.add),
                        s_st, waits=stw, tag=f"sca_{pi}_{sub}")
            tile_done[tt] = (pv, pv.v)
        pass_store_done = (s_st, s_st.v)
        wl_prev = (pv, pv.v)
    do("gpsimd", None, waits=[(s_st, s_st.v)])


from concourse.bass_utils import run_bass_kernel_spmd
from concourse.replica_groups import maybe_share_collective_output_space

_T = 4096
_NCORE = 8
_CAP = 10240
_RG = [list(range(8))]


def _gate_select(k, ge_ap, fb_ap, selc_ap, gates9_ap, uniq="gs", xlist_ap=None, B=2048,
                 xntok16_ap=None, augh_ap=None, sloti8=None):
    do, dma = k.do, k.dmago
    from concourse.bass_core_rust import IndirectOffsetOnAxis
    pv, ps_, pt = k.pg["vector"], k.pg["scalar"], k.pg["tensor"]
    U = uniq
    g256 = k.sb(f"{U}_g256", [128, 256], F32)
    g256b = k.sb(f"{U}_g256b", [128, 256], BF16)
    g256l = k.sb(f"{U}_g256l", [128, 256], BF16)
    g256t = k.sb(f"{U}_g256t", [128, 256], F32)
    geT = k.sb(f"{U}_geT", [128, 256], BF16)
    geTl = k.sb(f"{U}_geTl", [128, 256], BF16)
    selc = k.sb(f"{U}_selc", [128, 64], BF16)
    own32 = k.sb(f"{U}_own", [32, 128], F32)
    gcol = k.sb(f"{U}_gcol", [128, 32], F32)
    idb = k.sb(f"{U}_idb", [128, 128], BF16)
    idf = k.sb(f"{U}_idf", [128, 128], F32)
    PA = k.ps(f"{U}_PA", [128, 512], F32)
    PBb = k.ps(f"{U}_PBb", [128, 512], BF16)
    c_idb = k.const(f"{U}_idbc", np.eye(128, dtype=ml_dtypes.bfloat16))
    c_idf = k.const(f"{U}_idfc", np.eye(128, dtype=np.float32))
    s_l = k.sem(f"{U}_sl")
    s_lg = k.sem(f"{U}_slg")
    s_s = k.sem(f"{U}_ss")
    if xlist_ap is not None:
        mk_ = k.sb(f"{U}_mk", [128, 32], F32)
        mk16 = k.sb(f"{U}_mk16", [128, 32], BF16)
        posf = k.sb(f"{U}_posf", [128, 32], F32)
        ghilo = k.sb(f"{U}_ghilo", [128, 8 * 64], BF16)
        gf32 = k.sb(f"{U}_gf32", [128, 32], F32)
        xtile = [k.sb(f"{U}_xt{p}", [128, 1024], BF16) for p in range(2)]
        aug = [k.sb(f"{U}_aug{p}", [128, 1028], BF16) for p in range(2)]
        idhl = k.sb(f"{U}_idhl", [128, 64], BF16)
        pref = k.sb(f"{U}_pref", [128, 1028], BF16)
        rowp = k.sb(f"{U}_rowp", [1, 64], F32)
        triu16 = k.sb(f"{U}_triu16", [128, 128], BF16)
        ones16c = k.sb(f"{U}_ones16c", [128, 1], BF16)
        onesk1f2 = k.sb(f"{U}_onesk1f2", [1, 128], F32)
        PC = k.ps(f"{U}_PC", [128, 512], F32)
        c_triu = k.const(f"{U}_triuc", np.triu(np.ones((128, 128), np.float32)).astype(ml_dtypes.bfloat16))
        idhl_np = np.zeros((128, 64), np.float32)
        for cc2 in range(32):
            ids = cc2 * 128 + np.arange(128)
            idhl_np[:, cc2] = ids // 64
            idhl_np[:, 32 + cc2] = ids % 64
        c_idhl = k.const(f"{U}_idhlc", idhl_np.astype(ml_dtypes.bfloat16))
        s_pf = k.sem(f"{U}_spf")
        s_xl = [k.sem(f"{U}_sxl{p}") for p in range(2)]
        s_scat2 = [k.sem(f"{U}_sscat{p}") for p in range(2)]
        dma("sync", lambda e: e.dma_start(out=triu16[:, :], in_=c_triu[:, :]), s_l)
        dma("sync", lambda e: e.dma_start(out=idhl[:, :], in_=c_idhl[:, :]), s_l)
        do("gpsimd", lambda e: e.memset(ones16c[:], 1.0), inc=k.pg["gpsimd"])
        do("gpsimd", lambda e: e.memset(onesk1f2[:], 1.0), waits=[k.pg["gpsimd"]], inc=k.pg["gpsimd"])
        do("gpsimd", lambda e: e.memset(pref[:, :], 0.0), waits=[k.pg["gpsimd"]], inc=k.pg["gpsimd"])
        do("gpsimd", lambda e: e.memset(pref[:, 1024:1025], 64.0), waits=[k.pg["gpsimd"]], inc=k.pg["gpsimd"])
        gmem = (k.pg["gpsimd"], k.pg["gpsimd"].v)
        for e_ in range(8):
            for r0 in range(0, B + 1, 128):
                nr = min(128, B + 1 - r0)
                dma("gpsimd", lambda e, e_=e_, r0=r0, nr=nr: e.dma_start(
                    out=xlist_ap[e_][r0:r0 + nr, :], in_=pref[0:nr, :]), s_pf, waits=[gmem] if (e_ == 0 and r0 == 0) else [])
        prefill_done = (s_pf, s_pf.v)

    dma("sync", lambda e: e.dma_start(out=idb[:, :], in_=c_idb[:, :]), s_l)
    dma("sync", lambda e: e.dma_start(out=idf[:, :], in_=c_idf[:, :]), s_l)
    for j in range(2):
        dma("sync", lambda e, j=j: e.dma_start(out=selc[:, j * 32:(j + 1) * 32],
            in_=selc_ap[j * 128:(j + 1) * 128, :]), s_l)
    ld0 = (s_l, s_l.v)
    for i in range(9):
        src = fb_ap if i == 0 else ge_ap[i - 1]
        lg = dma("sync", lambda e, src=src: e.dma_start(out=g256[:, :], in_=src[:, :]), s_lg,
                 waits=[(pt, pt.v)])
        do("vector", lambda e: e.tensor_copy(out=g256b[:, :], in_=g256[:, :]),
           waits=[lg, (pv, pv.v)], inc=pv)
        do("vector", lambda e: e.tensor_copy(out=g256t[:, :], in_=g256b[:, :]),
           waits=[(pv, pv.v)], inc=pv)
        do("vector", lambda e: e.tensor_sub(out=g256t[:, :], in0=g256[:, :], in1=g256t[:, :]),
           waits=[(pv, pv.v)], inc=pv)
        do("vector", lambda e: e.tensor_copy(out=g256l[:, :], in_=g256t[:, :]),
           waits=[(pv, pv.v)], inc=pv)
        cast = (pv, pv.v)
        for j in range(2):
            do("tensor", lambda e, j=j: e.transpose(out=PBb[:, j * 128:(j + 1) * 128],
               in_=g256b[:, j * 128:(j + 1) * 128], identity=idb[:, :]),
               waits=[cast, ld0, (ps_, ps_.v)], inc=pt)
        do("scalar", lambda e: e.copy(out=geT[:, :], in_=PBb[:, 0:256]), waits=[(pt, pt.v)], inc=ps_)
        for j in range(2):
            do("tensor", lambda e, j=j: e.transpose(out=PBb[:, j * 128:(j + 1) * 128],
               in_=g256l[:, j * 128:(j + 1) * 128], identity=idb[:, :]),
               waits=[cast, ld0, (ps_, ps_.v)], inc=pt)
        do("scalar", lambda e: e.copy(out=geTl[:, :], in_=PBb[:, 0:256]), waits=[(pt, pt.v)], inc=ps_)
        gT = (ps_, ps_.v)
        def mmsel(e):
            for j in range(2):
                e.matmul(out=PA[0:32, 0:128], lhsT=selc[:, j * 32:(j + 1) * 32],
                         rhs=geT[:, j * 128:(j + 1) * 128], start=(j == 0), stop=False)
            for j in range(2):
                mm = e.matmul(out=PA[0:32, 0:128], lhsT=selc[:, j * 32:(j + 1) * 32],
                              rhs=geTl[:, j * 128:(j + 1) * 128], start=False, stop=(j == 1))
            return mm
        do("tensor", mmsel, waits=[gT], inc=pt)
        do("vector", lambda e: e.tensor_copy(out=own32[:, :], in_=PA[0:32, 0:128]),
           waits=[(pt, pt.v), (pv, pv.v), (s_s, s_s.v)], inc=pv)
        oc = (pv, pv.v)
        do("tensor", lambda e: e.transpose(out=PA[:, 128:160], in_=own32[:, :], identity=idf[0:32, 0:32]),
           waits=[oc], inc=pt)
        do("vector", lambda e: e.tensor_copy(out=gcol[:, :], in_=PA[:, 128:160]),
           waits=[(pt, pt.v), (pv, pv.v)], inc=pv)
        gc = (pv, pv.v)
        dma("gpsimd", lambda e, i=i: e.dma_start(out=gates9_ap[i, :, :], in_=gcol[:, :]),
            s_s, waits=[gc])
        if xlist_ap is not None and i > 0:
            e_ = i - 1
            V = lambda fn, w=(): do("vector", fn, waits=list(w) + [(pv, pv.v)], inc=pv) or (pv, pv.v)
            V(lambda e: e.tensor_scalar(out=mk_[:, :], in0=gcol[:, :], scalar1=0.0, scalar2=None,
               op0=ALU.is_gt), [gc])
            ecast = V(lambda e: e.tensor_copy(out=mk16[:, :], in_=mk_[:, :]))
            do("tensor", lambda e: e.matmul(out=PC[:, 0:32], lhsT=triu16[:, :], rhs=mk16[:, :],
               start=True, stop=True), waits=[ecast, gmem, (pt, pt.v)], inc=pt)
            do("tensor", lambda e: e.matmul(out=PC[0:1, 64:96], lhsT=ones16c[:, :], rhs=mk16[:, :],
               start=True, stop=True), waits=[ecast], inc=pt)
            cs = (pt, pt.v)
            V(lambda e: e.tensor_copy(out=rowp[:, 0:32], in_=PC[0:1, 64:96]), [cs])
            sh = 1
            cr, ot = 0, 32
            while sh < 32:
                V(lambda e, cr=cr, ot=ot, sh=sh: e.tensor_add(out=rowp[:, ot + sh:ot + 32],
                   in0=rowp[:, cr + sh:cr + 32], in1=rowp[:, cr:cr + 32 - sh]))
                V(lambda e, cr=cr, ot=ot, sh=sh: e.tensor_copy(out=rowp[:, ot:ot + sh],
                   in_=rowp[:, cr:cr + sh]))
                cr, ot = ot, cr
                sh *= 2
            V(lambda e, cr=cr: e.tensor_sub(out=rowp[:, cr:cr + 32], in0=rowp[:, cr:cr + 32],
               in1=PC[0:1, 64:96]))
            exd = (pv, pv.v)
            do("tensor", lambda e, cr=cr: e.matmul(out=PC[:, 128:160], lhsT=onesk1f2[:, :],
               rhs=rowp[:, cr:cr + 32], start=True, stop=True), waits=[exd], inc=pt)
            bc = (pt, pt.v)
            V(lambda e: e.tensor_copy(out=posf[:, :], in_=PC[:, 128:160]), [bc])
            V(lambda e: e.tensor_tensor(out=posf[:, :], in0=posf[:, :], in1=PC[:, 0:32], op=ALU.add))
            V(lambda e: e.tensor_scalar(out=posf[:, :], in0=posf[:, :], scalar1=float(-1 - B),
               scalar2=None, op0=ALU.add))
            V(lambda e: e.scalar_tensor_tensor(out=posf[:, :], in0=posf[:, :], scalar=1.0,
               in1=mk_[:, :], op0=ALU.mult, op1=ALU.mult))
            V(lambda e: e.tensor_scalar(out=posf[:, :], in0=posf[:, :], scalar1=float(B),
               scalar2=None, op0=ALU.add))
            sloti = sloti8[:, e_ * 32:(e_ + 1) * 32]
            sli = V(lambda e, sloti=sloti: e.tensor_copy(out=sloti, in_=posf[:, :]))
            ghi = ghilo[:, e_ * 64:e_ * 64 + 32]
            glo = ghilo[:, e_ * 64 + 32:e_ * 64 + 64]
            V(lambda e, ghi=ghi: e.tensor_copy(out=ghi, in_=gcol[:, :]))
            V(lambda e, ghi=ghi: e.tensor_copy(out=gf32[:, :], in_=ghi))
            V(lambda e: e.tensor_sub(out=gf32[:, :], in0=gcol[:, :], in1=gf32[:, :]))
            V(lambda e, glo=glo: e.tensor_copy(out=glo, in_=gf32[:, :]))
    if xlist_ap is not None:
        comp_done = (pv, pv.v)
        nst = [0, 0]
        for c in range(32):
            p2 = c % 2
            lxw = [comp_done] if c < 2 else [(s_scat2[0], (c - 1) * 4 * 16),
                                             (s_scat2[1], (c - 1) * 4 * 16)]
            lx = dma("sync", lambda e, c=c, p2=p2: e.dma_start(out=xtile[p2][:, :],
                     in_=xntok16_ap[c * 128:(c + 1) * 128, :]), s_xl[p2], waits=lxw)
            for par in range(2):
                aw = [lx, (pv, pv.v)]
                if nst[par] > 0:
                    aw.append((s_scat2[par], nst[par] * 16))
                do("vector", lambda e, p2=p2, par=par: e.tensor_copy(out=aug[par][:, 0:1024],
                   in_=xtile[p2][:, :]), waits=aw, inc=pv)
                do("vector", lambda e, c=c, par=par: e.tensor_copy(out=aug[par][:, 1024:1025],
                   in_=idhl[:, c:c + 1]), waits=[(pv, pv.v)], inc=pv)
                do("vector", lambda e, c=c, par=par: e.tensor_copy(out=aug[par][:, 1025:1026],
                   in_=idhl[:, 32 + c:33 + c]), waits=[(pv, pv.v)], inc=pv)
            for e_ in range(8):
                par = e_ % 2
                gw = [(pv, pv.v)]
                if nst[par] > 0:
                    gw.append((s_scat2[par], nst[par] * 16))
                do("vector", lambda e, c=c, e_=e_: e.tensor_copy(out=aug[e_ % 2][:, 1026:1027],
                   in_=ghilo[:, e_ * 64 + c:e_ * 64 + c + 1]), waits=gw, inc=pv)
                do("vector", lambda e, c=c, e_=e_: e.tensor_copy(out=aug[e_ % 2][:, 1027:1028],
                   in_=ghilo[:, e_ * 64 + 32 + c:e_ * 64 + 33 + c]), waits=[(pv, pv.v)], inc=pv)
                augd = (pv, pv.v)
                dma("gpsimd", lambda e, c=c, e_=e_: e.dma_start(
                    out=augh_ap[e_][c * 128:(c + 1) * 128, :],
                    in_=aug[e_ % 2][:, :]), s_scat2[par], waits=[augd])
                nst[par] += 1
    do("gpsimd", None, waits=[(s_s, s_s.v)])
    k._gsel_prefill_done = prefill_done if xlist_ap is not None else None


def _bg_scatters(k, augh_ap, xlist_ap, sloti8, augbg, s_ld, s_scat):
    from concourse.bass_core_rust import IndirectOffsetOnAxis
    nldA = [0, 0]
    nsc = [0, 0]
    for c in range(32):
        for e_ in range(8):
            n = c * 8 + e_
            par = n % 2
            lw = []
            if nsc[par] > 0:
                lw.append((s_scat[par], nsc[par] * 16))
            if c == 0 and e_ == 0 and k._gsel_prefill_done:
                lw.append(k._gsel_prefill_done)
            k.do("gpsimd", lambda e, c=c, e_=e_, par=par: e.dma_start(
                out=augbg[par][:, :], in_=augh_ap[e_][c * 128:(c + 1) * 128, :]),
                waits=lw, inc=s_ld[par], n=16)
            nldA[par] += 1
            k.do("gpsimd", lambda e, c=c, e_=e_, par=par: e.indirect_dma_start(
                out=xlist_ap[e_], out_offset=IndirectOffsetOnAxis(
                    sloti8[:, e_ * 32 + c:e_ * 32 + c + 1], 0),
                in_=augbg[par][:, :], in_offset=None),
                waits=[(s_ld[par], nldA[par] * 16)], inc=s_scat[par], n=16)
            nsc[par] += 1


def _s1_select(k, s1all_ap, selr_ap, s1own_ap, uniq="rs"):
    do, dma = k.do, k.dmago
    pv, ps_, pt = k.pg["vector"], k.pg["scalar"], k.pg["tensor"]
    U = uniq
    s1a = k.sb(f"{U}_s1a", [128, 1024], F32)
    s1b = k.sb(f"{U}_s1b", [128, 1024], BF16)
    selr = k.sb(f"{U}_selr", [128, 16], BF16)
    outt = k.sb(f"{U}_outt", [16, 1024], F32)
    PA = k.ps(f"{U}_PA", [128, 512], F32)
    s_l = k.sem(f"{U}_sl")
    s_s = k.sem(f"{U}_ss")
    l1 = dma("sync", lambda e: e.dma_start(out=s1a[:, :], in_=s1all_ap[:, :]), s_l)
    l2 = dma("sync", lambda e: e.dma_start(out=selr[:, :], in_=selr_ap[:, :]), s_l)
    do("vector", lambda e: e.tensor_copy(out=s1b[:, :], in_=s1a[:, :]), waits=[l1, (pv, pv.v)], inc=pv)
    cast = (pv, pv.v)
    for half in range(2):
        do("tensor", lambda e, half=half: e.matmul(out=PA[0:16, :], lhsT=selr[:, :],
           rhs=s1b[:, half * 512:(half + 1) * 512], start=True, stop=True),
           waits=[cast, l2] + ([(pv, pv.v)] if half else []), inc=pt)
        do("vector", lambda e, half=half: e.tensor_copy(out=outt[:, half * 512:(half + 1) * 512],
           in_=PA[0:16, :]), waits=[(pt, pt.v), (pv, pv.v)], inc=pv)
    dma("gpsimd", lambda e: e.dma_start(out=s1own_ap[:, :], in_=outt[:, :]), s_s,
        waits=[(pv, pv.v)])
    do("gpsimd", None, waits=[(s_s, s_s.v)])


def _build_fused():
    k = KB()
    dI, dO = "ExternalInput", "ExternalOutput"
    T = _T
    x = k.dram("x", [T, 1024], F32, kind=dI)
    wr8 = k.dram("wr8", [8, 1024], F32, kind=dI)
    A = {}
    for l in range(3):
        for n, shp in [("wq", [1024, 1024]), ("wk", [1024, 1024]), ("wv", [1024, 1024]),
                       ("wo", [1024, 1024]), ("bq", [1024]), ("bk", [1024]), ("bv", [1024]),
                       ("bo", [1024]), ("om", [64, 32]), ("g1", [1024]), ("b1", [1024]),
                       ("w1", [1024, 4096]), ("w2", [4096, 1024]), ("bf1", [4096]),
                       ("bf2", [1024]), ("g2", [1024]), ("b2", [1024])]:
            A[(n, l)] = k.dram(f"{n}{l}", shp, F32, kind=dI)
    M = {}
    for n, shp in [("w1s", [8, 1024, 4096]), ("b1s", [8, 4096]), ("w2s", [8, 4096, 1024]),
                   ("b2s", [8, 1024]), ("fw1", [1024, 4096]), ("fb1", [4096]),
                   ("fw2", [4096, 1024]), ("fb2", [1024])]:
        M[n] = k.dram(n, shp, F32, kind=dI)
    L1 = {}
    for n, shp in [("qw", [1024, 1024]), ("qb", [1024]), ("kw", [1024, 1024]),
                   ("vw", [1024, 1024]), ("vb", [1024]), ("ow", [1024, 1024]), ("ob", [1024]),
                   ("lna", [1024]), ("lnb", [1024]), ("lnc", [64]), ("lnd", [64]),
                   ("lqw", [64, 64]), ("lqb", [64]), ("lkw", [64, 64]), ("lkb", [64]),
                   ("lvw", [64, 64]), ("lvb", [64]),
                   ("fr", [1024, 1024]), ("fk", [1024, 1024]), ("fv", [1024, 1024]),
                   ("mur", [1024]), ("muk", [1024])]:
        L1[n] = k.dram(f"z{n}", shp, F32, kind=dI)
    selc = k.dram("selc", [256, 32], BF16, kind=dI)
    selr = k.dram("selr", [128, 16], BF16, kind=dI)
    y = k.dram("y", [T, 1024], F32, kind=dO)

    cur = [k.dram(f"curS{i}", [1024, T], F32) for i in range(2)]
    h1S = k.dram("h1S", [1024, T], F32)
    h1bS = k.dram("h1bS", [1024, T], BF16)
    xnf16S = k.dram("xnf16S", [1024, T], BF16)
    xntok16S = k.dram("xntok16S", [T + 128, 1024], BF16)
    xlistS = [k.dram(f"xlistS{e}", [1537, 1028], BF16) for e in range(8)]
    aughS = [k.dram(f"aughS{e}", [4096, 1028], BF16) for e in range(8)]
    lg_in = k.dram("lg_inS", [T, 8], F32)
    ag_space = maybe_share_collective_output_space("AllGather", _RG, prefer_local=True)
    lg_all = k.dram("lg_allS", [T * 8, 8], F32, kind="Internal")
    geS = k.dram("geS", [8, 128, 256], F32)
    fbS = k.dram("fbS", [128, 256], F32)
    gates9S = k.dram("gates9S", [9, 128, 32], F32)
    yaccS = k.dram("yaccS", [T + 128, 1024], F32)
    x1S = k.dram("x1S", [T, 1024], F32)
    xn1fmS = k.dram("xn1fmS", [1024, T], BF16)
    x16ownS = k.dram("x16ownS", [16, 1024], F32)
    x16allS = k.dram("x16allS", [128, 1024], F32)
    s1allS = k.dram("s1allS", [128, 1024], F32)
    s1ownS = k.dram("s1ownS", [16, 1024], F32)
    g1S = k.dram("g1S", [T, 1024], BF16)

    k.phase = "prelude"
    sloti8 = k.sb("sloti8", [128, 256], I32)
    augbg = [k.sb(f"augbg{p}", [128, 1028], BF16) for p in range(2)]
    s_scat = [k.sem(f"s_scat{p}") for p in range(2)]
    s_bgl = [k.sem(f"s_bgl{p}") for p in range(2)]
    k.phase = "prelude"
    with k.scope():
        ph_prelude(k, T, x, wr8, cur[0], xnf16S, lg_in, xntok16_ap=xntok16S)
    cagg1 = k.sem("cagg1")
    k.do("gpsimd", lambda e: e.collective_compute("AllGather", ALU.bypass,
         replica_groups=_RG, ins=[lg_in.opt()], outs=[lg_all.opt()]), inc=cagg1, n=1)
    k.barrier()
    k.phase = "route"
    with k.scope():
        ph_moe_route(k, T * 8, lg_all, geS, fbS, _CAP)
    k.phase = "gsel"
    with k.scope():
        _gate_select(k, geS, fbS, selc, gates9S, xlist_ap=xlistS, B=1536,
                     xntok16_ap=xntok16S, augh_ap=aughS, sloti8=sloti8)
    k.phase = "bgscat"
    _bg_scatters(k, aughS, xlistS, sloti8, augbg, s_bgl, s_scat)
    k.barrier_exclude = {"s_scat0", "s_scat1", "s_bgl0", "s_bgl1"}
    k.barrier()
    for l in range(3):
        k.phase = f"attnA{l}"
        with k.scope():
            ph_l0_attnA(k, T, cur[l % 2], A[("wq", l)], A[("wk", l)], A[("wv", l)], A[("wo", l)],
                        A[("bq", l)], A[("bk", l)], A[("bv", l)], A[("bo", l)], A[("om", l)],
                        A[("g1", l)], A[("b1", l)], h1S, h1bS, uniq=f"a{l}")
        k.phase = f"ffnB{l}"
        with k.scope():
            ph_l0_ffnB(k, T, h1S, A[("w1", l)], A[("w2", l)], A[("bf1", l)], A[("bf2", l)],
                       A[("g2", l)], A[("b2", l)], cur[(l + 1) % 2], uniq=f"f{l}")
    sfinal = cur[1]
    k.barrier_exclude = set()
    k.barrier()
    k.phase = "moe"
    with k.scope():
        ph_moe_apply_c(k, T, xnf16S, xlistS, 1536, M["w1s"], M["b1s"], M["w2s"],
                       M["b2s"], M["fw1"], M["fb1"], M["fw2"], M["fb2"], gates9S, yaccS)
    k.phase = "comb"
    with k.scope():
        ph_b0_combine(k, T, x, sfinal, yaccS, x1S, xn1fmS, x16ownS)
    cagg2 = k.sem("cagg2")
    k.do("gpsimd", lambda e: e.collective_compute("AllGather", ALU.bypass,
         replica_groups=_RG, ins=[x16ownS.opt()], outs=[x16allS.opt()]), inc=cagg2, n=1)
    k.barrier()
    k.phase = "l1s"
    with k.scope():
        ph_l1_small(k, x16allS, L1["qw"], L1["qb"], L1["kw"], L1["vw"], L1["vb"], L1["ow"],
                    L1["ob"], L1["lna"], L1["lnb"], L1["lnc"], L1["lnd"], L1["lqw"], L1["lqb"],
                    L1["lkw"], L1["lkb"], L1["lvw"], L1["lvb"], s1allS)
    k.phase = "s1sel"
    with k.scope():
        _s1_select(k, s1allS, selr, s1ownS)
    k.phase = "l1f"
    with k.scope():
        ph_l1_ffn(k, T, xn1fmS, L1["fr"], L1["fk"], L1["fv"], L1["mur"], L1["muk"], g1S)
    k.phase = "final"
    with k.scope():
        ph_final(k, T, x1S, g1S, s1ownS, L1["ob"], y)
    return k.build()


def _ca(a):
    return np.ascontiguousarray(a)


def kernel(**inp):
    x0 = np.asarray(inp["L0_attn_x"])
    nc = _build_fused()
    maps = []
    for b in range(_NCORE):
        m = {"x": _ca(x0[b]), "wr8": _ca(np.asarray(inp["L0_ffn_Wr"]))}
        for l in range(3):
            for (dn, sn) in [("wq", "Wq"), ("wk", "Wk"), ("wv", "Wv"), ("wo", "Wo"),
                             ("bq", "bq"), ("bk", "bk"), ("bv", "bv"), ("bo", "bo"),
                             ("om", "omega"), ("g1", "ln1_g"), ("b1", "ln1_b"),
                             ("w1", "W1"), ("w2", "W2"), ("bf1", "bf1"), ("bf2", "bf2"),
                             ("g2", "ln2_g"), ("b2", "ln2_b")]:
                m[f"{dn}{l}"] = _ca(np.asarray(inp[f"L0_attn_{sn}"])[l])
        for (dn, sn) in [("w1s", "W1"), ("b1s", "b1"), ("w2s", "W2"), ("b2s", "b2"),
                         ("fw1", "fw1"), ("fb1", "fb1"), ("fw2", "fw2"), ("fb2", "fb2")]:
            m[dn] = _ca(np.asarray(inp[f"L0_ffn_{sn}"]))
        for (dn, sn) in [("qw", "L1_attn_qw"), ("qb", "L1_attn_qb"), ("kw", "L1_attn_kw"),
                         ("vw", "L1_attn_vw"), ("vb", "L1_attn_vb"), ("ow", "L1_attn_ow"),
                         ("ob", "L1_attn_ob"), ("lna", "L1_attn_lna_w"), ("lnb", "L1_attn_lnb_w"),
                         ("lnc", "L1_attn_lnc_w"), ("lnd", "L1_attn_lnd_w"), ("lqw", "L1_attn_lqw"),
                         ("lqb", "L1_attn_lqb"), ("lkw", "L1_attn_lkw"), ("lkb", "L1_attn_lkb"),
                         ("lvw", "L1_attn_lvw"), ("lvb", "L1_attn_lvb"), ("fr", "L1_ffn_W_r"),
                         ("fk", "L1_ffn_W_k"), ("fv", "L1_ffn_W_v"), ("mur", "L1_ffn_mu_r"),
                         ("muk", "L1_ffn_mu_k")]:
            m[f"z{dn}"] = _ca(np.asarray(inp[sn]))
        sc = np.zeros((256, 32), ml_dtypes.bfloat16)
        for j in range(32):
            sc[b * 32 + j, j] = 1
        m["selc"] = sc
        sr = np.zeros((128, 16), ml_dtypes.bfloat16)
        for j in range(16):
            sr[b * 16 + j, j] = 1
        m["selr"] = sr
        maps.append(m)
    res = run_bass_kernel_spmd(nc, maps, core_ids=list(range(_NCORE)))
    return np.stack([res.results[b]["y"] for b in range(_NCORE)], axis=0)
